# version 10
# baseline (speedup 1.0000x reference)
import numpy as np
import ml_dtypes
import concourse.bass as bass
import concourse.mybir as mybir
from concourse.bass_utils import run_bass_kernel_spmd

F32 = mybir.dt.float32
BF16 = mybir.dt.bfloat16
AF = mybir.ActivationFunctionType
ALU = mybir.AluOpType
BF = ml_dtypes.bfloat16

T = 512
NTILE = 4
D = 2048
QA, QB = 256, 128
NHA, NHB = 32, 64
PA, PB = 128, 64
NST = 64

SBUF_LIMIT = 229247
SSD_CUT = 0


def _lhsT_strips(wT):
    K, N = wT.shape
    kc, ns = K // 128, N // 128
    A = wT.reshape(kc, 128, ns, 128)
    return np.ascontiguousarray(A.transpose(2, 1, 0, 3)).reshape(ns, 128, K)


def _rhs_strips(w, sw):
    K, N = w.shape
    kc, ns = K // 128, N // sw
    A = w.reshape(kc, 128, ns, sw)
    return np.ascontiguousarray(A.transpose(2, 1, 0, 3)).reshape(ns, 128, kc * sw)


def _col128(v):
    return np.ascontiguousarray(v.reshape(-1, 128).T)


def stage_weights(ins):
    st = {}
    f32 = np.float32

    wa = np.pad(ins["L0_attn_a_in_proj_w"].astype(f32), ((0, 96), (0, 0)))
    st["wA"] = _lhsT_strips(np.ascontiguousarray(wa.T))
    wb = np.pad(ins["L0_attn_b_W_in"].astype(f32), ((0, 64), (0, 0)))
    st["wB"] = _lhsT_strips(np.ascontiguousarray(wb.T))
    st["wOa"] = _lhsT_strips(np.ascontiguousarray(ins["L0_attn_a_out_proj_w"].astype(f32).T))
    st["wOb"] = _lhsT_strips(np.ascontiguousarray(ins["L0_attn_b_W_out"].astype(f32).T))
    st["wG"] = _rhs_strips(ins["L0_ffn_w_gate"].astype(f32), 256)
    st["wU"] = _rhs_strips(ins["L0_ffn_w_up"].astype(f32), 256)
    st["wD"] = _lhsT_strips(ins["L0_ffn_w_down"].astype(f32))

    st["bias_a"] = _col128(np.pad(ins["L0_attn_a_in_proj_b"].astype(f32), (0, 96)))
    st["obias_a"] = _col128(ins["L0_attn_a_out_proj_b"].astype(f32))
    cwa = ins["L0_attn_a_conv_w"].astype(f32).reshape(4224, 4)
    st["convw_a"] = np.ascontiguousarray(cwa.reshape(33, 128, 4).transpose(1, 0, 2)).reshape(128, 132)
    st["convb_a"] = _col128(ins["L0_attn_a_conv_b"].astype(f32))
    cwb = ins["L0_attn_b_conv_w"].astype(f32)
    st["convw_b"] = np.ascontiguousarray(cwb.reshape(33, 128, 4).transpose(1, 0, 2)).reshape(128, 132)
    st["convb_b"] = _col128(ins["L0_attn_b_conv_b"].astype(f32))
    st["lnw_a"] = _col128(ins["L0_attn_a_ln_w"].astype(f32))
    st["lnb_a"] = _col128(ins["L0_attn_a_ln_b"].astype(f32))
    st["normw_b"] = _col128(ins["L0_attn_b_norm_w"].astype(f32))
    st["dprm"] = _col128(np.repeat(ins["L0_attn_b_D_param"].astype(f32), PB))
    st["dtb_a"] = np.pad(ins["L0_attn_a_dt_bias"].astype(f32), (0, 32)).reshape(64, 1)
    st["alog_a"] = np.pad(ins["L0_attn_a_A_log"].astype(f32), (0, 32)).reshape(64, 1)
    st["dtb_b"] = ins["L0_attn_b_dt_bias"].astype(f32).reshape(64, 1)
    st["alog_b"] = ins["L0_attn_b_A_log"].astype(f32).reshape(64, 1)

    st["triu"] = np.zeros((128, 512), f32)
    for stile in range(2):
        for p in range(128):
            for_l = np.arange(256)
            st["triu"][p, stile * 256:(stile + 1) * 256] = (stile * 128 + p <= for_l).astype(f32)
    st["ident_bf"] = np.eye(128, dtype=BF)
    st["ident_f"] = np.eye(64, dtype=f32)
    st["ones_bf"] = np.ones((1, 128), dtype=BF)
    st["ones_f"] = np.ones((1, 128), dtype=f32)
    st["onescol_bf"] = np.ones((128, 1), dtype=BF)
    st["biasvals"] = np.tile(np.array([[1e-5, 1.0]], dtype=f32), (128, 1))
    return st



class KB:

    ENGS = ("sync", "gpsimd", "tensor", "vector", "scalar")

    def __init__(self, nc, upto="full"):
        self.nc = nc
        self.upto = upto
        self.streams = {e: [] for e in self.ENGS}
        self.c = {"W": 0, "S": 0, "P": 0, "V": 0, "A": 0, "O": 0, "G": 0}
        self.sem = {}
        self.lastw = {e: {} for e in self.ENGS}
        self.sbuf_ptr = 0
        self.tensors = {}
        self.wait_log = []
        self.phase = "pre"
        self.pe_labels = []

    def alloc(self, name, shape, dtype, nbytes=None):
        t = self.nc.alloc_sbuf_tensor_at(name, list(shape), dtype, offset=self.sbuf_ptr)
        if nbytes is None:
            esz = 2 if dtype == BF16 else 4
            nbytes = int(np.prod(shape[1:])) * esz
        self.sbuf_ptr += (nbytes + 31) // 32 * 32
        assert self.sbuf_ptr <= SBUF_LIMIT, f"SBUF overflow at {name}: {self.sbuf_ptr}"
        self.tensors[name] = t
        return t

    def alloc_at(self, name, shape, dtype, offset):
        t = self.nc.alloc_sbuf_tensor_at(name, list(shape), dtype, offset=offset)
        self.tensors[name] = t
        return t

    def emit(self, eng, fn):
        self.streams[eng].append(fn)

    def wait(self, eng, sem, val):
        if val <= 0:
            return
        if self.lastw[eng].get(sem, -1) >= val:
            return
        self.lastw[eng][sem] = val
        self.wait_log.append((eng, sem, val))
        h = self.sem[sem]
        self.emit(eng, lambda e, h=h, v=val: e.wait_ge(h, v))

    def bump(self, sem, d):
        self.c[sem] += d
        return self.c[sem]

    def run(self, inst, sem=None, d=1):
        if sem is None:
            return inst
        h = self.sem[sem]
        self.bump(sem, d)

        def g(e, inst=inst, h=h, d=d):
            inst(e).then_inc(h, d)
        return g

    def op(self, eng, inst, sem=None, d=1, waits=()):
        for (s, v) in waits:
            self.wait(eng, s, v)
        self.emit(eng, self.run(inst, sem, d))

    def barrier(self, engs=("tensor", "vector", "scalar", "sync")):
        snap = [("P", self.c["P"]), ("V", self.c["V"]), ("A", self.c["A"]),
                ("S", self.c["S"])]
        for e in engs:
            for s, v in snap:
                self.wait(e, s, v)

    def check_waits(self):
        bad = [(e, s, v, self.c[s]) for (e, s, v) in self.wait_log if v > self.c[s]]
        assert not bad, f"waits exceeding final counts (hang): {bad[:10]}"

    def finalize_into_block(self, block):
        for name, dec in (("sync", block.sync), ("gpsimd", block.gpsimd),
                          ("tensor", block.tensor), ("vector", block.vector),
                          ("scalar", block.scalar)):
            fns = self.streams[name]

            def make(fns):
                def _(eng):
                    for f in fns:
                        f(eng)
                return _
            dec(make(fns))


BASE = 16512
OFF_X = BASE + 0
OFF_S = BASE + 32768
OFF_HSP = BASE + 49152
OFF_ZB = BASE + 114688
OFF_FB = BASE + 155648
OFF_SCRAP = BASE + 172032
OFF_CONST = BASE + 188416


def build_core(upto="full"):
    nc = bass.Bass("TRN2", target_bir_lowering=False, detect_race_conditions=False)
    kb = KB(nc, upto)

    u = nc.dram_tensor("u", [2048, 2048], F32, kind="ExternalInput").ap()
    out = nc.dram_tensor("out", [2048, 2048], F32, kind="ExternalOutput").ap()
    W = {}
    for nm, shp in [("wA", [66, 128, 2048]), ("wB", [66, 128, 2048]),
                    ("wOa", [16, 128, 4096]), ("wOb", [16, 128, 4096]),
                    ("wG", [32, 128, 4096]), ("wU", [32, 128, 4096]),
                    ("wD", [16, 128, 8192])]:
        W[nm] = nc.dram_tensor(nm, shp, F32, kind="ExternalInput").ap()
    P = {}
    for nm, shp, dt in [("bias_a", [128, 66], F32), ("obias_a", [128, 16], F32),
                        ("convw_a", [128, 132], F32), ("convb_a", [128, 33], F32),
                        ("convw_b", [128, 132], F32), ("convb_b", [128, 33], F32),
                        ("lnw_a", [128, 32], F32), ("lnb_a", [128, 32], F32),
                        ("normw_b", [128, 32], F32), ("dprm", [128, 32], F32),
                        ("dtb_a", [64, 1], F32), ("alog_a", [64, 1], F32),
                        ("dtb_b", [64, 1], F32), ("alog_b", [64, 1], F32),
                        ("triu", [128, 512], F32), ("ident_bf", [128, 128], BF16),
                        ("ident_f", [64, 64], F32), ("ones_bf", [1, 128], BF16),
                        ("ones_f", [1, 128], F32), ("onescol_bf", [128, 1], BF16),
                        ("biasvals", [128, 2], F32)]:
        P[nm] = nc.dram_tensor(nm, shp, dt, kind="ExternalInput").ap()
    dscr = nc.dram_tensor("dscr", [2, 64, 256], F32, kind="Internal").ap()

    A = kb.alloc_at
    x_sb = A("x_sb", [128, 8192], BF16, OFF_X)
    S_bb = A("S_bb", [64, 4096], BF16, OFF_X + 16384)
    C_sc2 = A("C_sc2", [64, 2048], BF16, OFF_X + 24576)
    hst2 = A("hst2", [128, 1024], BF16, OFF_FB + 12288)
    S_ab = A("S_ab", [128, 4096], BF16, OFF_S)
    raw = A("raw", [128, 33 * 516], BF16, OFF_HSP)
    wslotA = A("wslotA", [128, 6 * 2048], BF16, OFF_HSP + 34816)
    conv = A("conv", [128, 33 * 512], BF16, OFF_HSP + 34080)
    X_tm = A("X_tm", [128, 2 * 4096], BF16, OFF_HSP)
    Mbuf = A("Mbuf", [128, 2 * 4096], BF16, OFF_HSP + 16384)
    wslotO = A("wslotO", [128, 2 * 4096], BF16, OFF_HSP)
    sqb = A("sqb", [128, 16384], BF16, OFF_HSP + 32768)
    hsp_fm = A("hsp_fm", [128, 64 * 512], BF16, OFF_HSP)
    wslotU = A("wslotU", [128, 2 * 4096], BF16, OFF_SCRAP)
    eflat = A("eflat", [1, 2048], BF16, OFF_ZB + 4096)
    z = A("z", [128, 16384], BF16, OFF_ZB + 8192)
    xnorm2 = A("xnorm2", [128, 8192], BF16, OFF_ZB + 8192)
    wslotG = A("wslotG", [128, 2 * 4096], BF16, OFF_ZB + 24576)
    wslotD = A("wslotD", [128, 2 * 8192], BF16, OFF_ZB + 8192)
    xnorm = A("xnorm", [128, 8192], BF16, OFF_FB)
    Ds = A("Ds", [128, 2048], F32, OFF_FB)
    ca_bc = A("ca_bc", [128, 2048], F32, OFF_FB + 8192)
    xcg = A("xcg", [128, 4096], BF16, OFF_FB)
    zs = A("zs", [128, 8192], BF16, OFF_FB)
    dsk = A("dsk", [128, 4096], BF16, OFF_FB + 8192)
    ffs = A("ffs", [128, 6144], BF16, OFF_FB)
    yfm = A("yfm", [128, 8192], BF16, OFF_SCRAP)

    kb.sbuf_ptr = OFF_CONST
    cb = {}
    for nm in ("triu", "ident_bf", "ident_f", "ones_bf", "ones_f", "onescol_bf",
               "biasvals", "bias_a", "obias_a", "convw_a", "convb_a", "convw_b",
               "convb_b", "lnw_a", "lnb_a", "normw_b", "dprm", "dtb_a", "alog_a",
               "dtb_b", "alog_b"):
        cb[nm] = kb.alloc(nm + "_s", list(P[nm].shape), P[nm].dtype)
    negA_a = kb.alloc("negA_a", [64, 1], F32)
    negA_b = kb.alloc("negA_b", [64, 1], F32)
    tail_a = kb.alloc("tail_a", [128, 99], BF16)
    tail_b = kb.alloc("tail_b", [128, 99], BF16)
    dtw1_off = kb.sbuf_ptr
    dtw1 = kb.alloc("dtw1", [64, 512], F32)
    ytm = A("ytm", [128, 1024], BF16, dtw1_off)
    lndt = kb.alloc("lndt", [64, 512], F32)
    afm = kb.alloc("afm", [64, 512], F32)
    cafm_off = kb.sbuf_ptr
    cafm = kb.alloc("cafm", [64, 256], F32)
    cwork_off = kb.sbuf_ptr
    cwork = kb.alloc("cwork", [64, 256], F32)
    cwork2 = kb.alloc("cwork2", [64, 256], F32)
    mubc = A("mubc", [128, 512], BF16, cafm_off)
    rsbc = A("rsbc", [128, 512], BF16, cwork_off)
    a_tm = A("a_tm", [128, 64], F32, OFF_ZB + 2368)
    cs_tm = A("cs_tm", [128, 128], F32, OFF_ZB + 2624)
    coeff_tm = A("coeff_tm", [128, 128], F32, OFF_ZB + 3136)
    B_tm = kb.alloc("B_tm", [128, 128], BF16)
    C_sc = kb.alloc("C_sc", [128, 512], BF16)
    E_fm = kb.alloc("E_fm", [64, 256], BF16)

    CBTm = kb.alloc("CBTm", [128, 512], BF16)
    C0 = kb.alloc("C0", [128, 256], BF16)
    stats_off = kb.sbuf_ptr
    stats = kb.alloc("stats", [1, 1024], F32)
    dtbuf = A("dtbuf", [128, 512], F32, stats_off)
    esrow = A("esrow", [1, 64], F32, stats_off + 2048)
    esbc = A("esbc", [128, 64], F32, stats_off + 2304)
    assert kb.sbuf_ptr <= 229247, kb.sbuf_ptr

    nc.bump_sbuf(kb.sbuf_ptr - 16512)
    PSA = nc.alloc_psum_tensor("PSA", [128, 2048], F32)
    PSB = nc.alloc_psum_tensor("PSB", [128, 2048], F32)
    kb.T = dict(locals())
    return nc, kb



def rview(ap, spec, **kw):
    return ap.rearrange(spec, **kw)


def bc3(ap2d, inner):
    return ap2d.rearrange("p (h o) -> p h o", o=1).broadcast_to(
        [ap2d.shape[0], ap2d.shape[1], inner])


def bc_row(ap2d, mid):
    return ap2d.rearrange("p (o n) -> p o n", o=1).broadcast_to(
        [ap2d.shape[0], mid, ap2d.shape[1]])


class Prog:
    def __init__(self, nc, kb):
        self.nc = nc
        self.kb = kb
        self.T = kb.T
        for s in ("W", "S", "P", "V", "A", "O", "G"):
            kb.sem[s] = nc.alloc_semaphore(name="sem" + s)
        self.slot_free = {}
        self.bank_epoch = {}
        self.bank_read = {}
        self.epoch_ctr = 0
        self.ca_slot_cp = {}
        self.xev_acp = {}
        self.conv_done_cp = 0
        self.mfree_cp = {}
        self.phase_gate_p = 0
        self.ebc_free = {}
        self.ytm_free = 0
        self.s_upd_cp = 0

    def new_epoch(self):
        self.epoch_ctr += 1
        return self.epoch_ctr

    @staticmethod
    def _banks(out):
        off = int(out.offset)
        ncols = int(out.ap[-1][1])
        pbase = off // 2048
        b0 = (off % 2048) // 512
        b1 = ((off % 2048) + ncols - 1) // 512
        return [(pbase, b) for b in range(b0, b1 + 1)]

    def mm(self, tname, out, lhsT, rhs, epoch, stop=True, sem=None, waits=()):
        kb = self.kb
        banks = self._banks(out)
        start = False
        xw = list(waits)
        for (pb_, b) in banks:
            key = (tname, pb_, b)
            if self.bank_epoch.get(key) != epoch:
                start = True
                self.bank_epoch[key] = epoch
                for s, cp in self.bank_read.pop((tname, b), {}).items():
                    xw.append((s, cp))
        assert len(banks) == 1 or not start or True
        kb.pe_labels.append(kb.phase)
        kb.op("tensor",
              lambda e, o=out, l=lhsT, r=rhs, st=start, sp=stop:
              e.matmul(out=o, lhsT=l, rhs=r, start=st, stop=sp,
                       skip_group_check=True),
              sem=sem, waits=xw)

    def note_read(self, tname, out_ap, sem, cp):
        for (pb_, b) in self._banks(out_ap):
            d = self.bank_read.setdefault((tname, b), {})
            d[sem] = max(d.get(sem, 0), cp)

    def evac(self, eng, fn, tname, src_ap, sem, waits=()):
        kb = self.kb
        kb.op(eng, fn, sem=sem, waits=waits)
        self.note_read(tname, src_ap, sem, kb.c[sem])
        return kb.c[sem]

    def reset_slots(self, *prefixes):
        for k in list(self.slot_free):
            if k[0] in prefixes:
                del self.slot_free[k]

    def load_strip(self, wsrc, sidx, slot_ap, slot_key):
        kb = self.kb
        waits = []
        if self.slot_free.get(slot_key):
            waits.append(self.slot_free[slot_key])
        else:
            waits += [("P", kb.c["P"]), ("V", kb.c["V"]), ("A", kb.c["A"])]
        if sidx < 6:
            waits.append(("P", self.phase_gate_p))
        kb.op("gpsimd",
              lambda e, d=slot_ap, s=wsrc[sidx]: e.dma_start(out=d, in_=s),
              sem="W", d=16, waits=waits)
        return kb.c["W"]

    def preamble(self):
        kb, T = self.kb, self.T
        P, cb = T["P"], T["cb"]
        for nm in cb:
            kb.op("gpsimd", lambda e, d=cb[nm][:], s=P[nm]: e.dma_start(out=d, in_=s),
                  sem="W", d=16)
        wcp = kb.c["W"]
        for (src, dst) in ((cb["alog_a"], T["negA_a"]), (cb["alog_b"], T["negA_b"])):
            kb.op("scalar", lambda e, s=src[:], d=dst[:]: e.activation(
                out=d, in_=s, func=AF.Exp), sem="A", waits=[("W", wcp)])
        acp = kb.c["A"]
        for dst in (T["negA_a"], T["negA_b"]):
            kb.op("vector", lambda e, d=dst[:]: e.tensor_scalar(
                out=d, in0=d, scalar1=-1.0, scalar2=None, op0=ALU.mult),
                sem="V", waits=[("A", acp)])
        for dst in (T["tail_a"], T["tail_b"]):
            kb.op("vector", lambda e, d=dst[:]: e.memset(d, 0.0), sem="V")
        kb.op("vector", lambda e: e.memset(T["S_ab"][:], 0.0), sem="V")
        kb.op("vector", lambda e: e.memset(T["S_bb"][:], 0.0), sem="V")

    def ln_x(self, dst):
        kb, T = self.kb, self.T
        kb.phase = "ln"
        x_sb, sqb, stats = T["x_sb"], T["sqb"], T["stats"]
        PSB = T["PSB"]
        onescol = T["cb"]["onescol_bf"]
        kb.barrier()
        self.phase_gate_p = kb.c["P"]
        vcp0 = kb.c["V"]
        kb.op("sync", lambda e, d=dst[:], s=x_sb[:]: e.dma_start(out=d, in_=s),
              sem="S", d=16, waits=[("V", vcp0), ("P", kb.c["P"])])
        ccp = kb.c["S"]
        ep = self.new_epoch()
        for dc in range(16):
            self.mm("PSB", PSB[0:1, 0:512], onescol[:], dst[:, dc * 512:(dc + 1) * 512],
                    ep, stop=(dc == 15), sem="P" if dc == 15 else None,
                    waits=[("S", ccp)] if dc == 0 else ())
        kb.op("scalar", lambda e, d=sqb[:, 0:8192], s=dst[:]: e.activation(
            out=d, in_=s, func=AF.Square), sem="A", waits=[("S", ccp)])
        sqcp = kb.c["A"]
        ep2 = self.new_epoch()
        for dc in range(16):
            self.mm("PSB", PSB[0:1, 512:1024], onescol[:],
                    sqb[:, dc * 512:(dc + 1) * 512],
                    ep2, stop=(dc == 15), sem="P" if dc == 15 else None,
                    waits=[("A", sqcp)] if dc == 0 else ())
        pcp = kb.c["P"]
        self._stats(pcp, sub_mean=True, n=2048.0)
        self._normalize(dst, 16, 512, sub_mean=True)

    def _stats(self, pcp, sub_mean, n):
        kb, T = self.kb, self.T
        stats, PSB = T["stats"], T["PSB"]
        self.evac("vector", lambda e: e.tensor_scalar(
            out=stats[0:1, 0:512], in0=PSB[0:1, 0:512], scalar1=1.0 / n,
            scalar2=None, op0=ALU.mult), "PSB", PSB[0:1, 0:512], "V",
            waits=[("P", pcp)])
        if sub_mean:
            kb.op("vector", lambda e: e.tensor_tensor(
                out=stats[0:1, 512:1024], in0=stats[0:1, 0:512],
                in1=stats[0:1, 0:512], op=ALU.mult), sem="V")
            self.evac("vector", lambda e: e.scalar_tensor_tensor(
                out=stats[0:1, 512:1024], in0=PSB[0:1, 512:1024], scalar=1.0 / n,
                in1=stats[0:1, 512:1024], op0=ALU.mult, op1=ALU.subtract),
                "PSB", PSB[0:1, 512:1024], "V", waits=[("P", pcp)])
        else:
            self.evac("vector", lambda e: e.tensor_scalar(
                out=stats[0:1, 512:1024], in0=PSB[0:1, 512:1024], scalar1=1.0 / n,
                scalar2=None, op0=ALU.mult), "PSB", PSB[0:1, 512:1024], "V",
                waits=[("P", pcp)])
        vcp = kb.c["V"]
        bv = T["cb"]["biasvals"]
        kb.op("scalar", lambda e: e.activation(
            out=stats[0:1, 512:1024], in_=stats[0:1, 512:1024], func=AF.Ln,
            bias=bv[0:1, 0:1]), sem="A", waits=[("V", vcp)])
        kb.op("scalar", lambda e: e.activation(
            out=stats[0:1, 512:1024], in_=stats[0:1, 512:1024], func=AF.Exp,
            scale=-0.5), sem="A")

    def _normalize(self, dst, nblk, blk, sub_mean, zview=None):
        kb, T = self.kb, self.T
        stats, PSA, mubc, rsbc = T["stats"], T["PSA"], T["mubc"], T["rsbc"]
        onesf = T["cb"]["ones_f"]
        acp0 = kb.c["A"]
        ep = self.new_epoch()
        if sub_mean:
            self.mm("PSA", PSA[:, 0:512], onesf[:],
                    stats[0:1, 0:512], ep, waits=[("A", acp0)])
        self.mm("PSA", PSA[:, 512:1024], onesf[:],
                stats[0:1, 512:1024], ep, sem="P", waits=[("A", acp0)])
        pcp = kb.c["P"]
        if sub_mean:
            self.evac("scalar", lambda e: e.copy(out=mubc[:], in_=PSA[:, 0:512]),
                      "PSA", PSA[:, 0:512], "A", waits=[("P", pcp)])
        self.evac("scalar", lambda e: e.copy(out=rsbc[:], in_=PSA[:, 512:1024]),
                  "PSA", PSA[:, 512:1024], "A", waits=[("P", pcp)])
        acp = kb.c["A"]
        if zview is None:
            dv = rview(dst[:, 0:nblk * blk], "p (kc t) -> p kc t", t=blk)
            mv = bc_row(mubc[:], nblk)
            rv = bc_row(rsbc[:], nblk)
        else:
            dv, mv, rv = zview
        if sub_mean:
            kb.op("vector", lambda e, dv=dv, mv=mv: e.tensor_tensor(
                out=dv, in0=dv, in1=mv, op=ALU.subtract), sem="V",
                waits=[("A", acp)])
        kb.op("vector", lambda e, dv=dv, rv=rv: e.tensor_tensor(
            out=dv, in0=dv, in1=rv, op=ALU.mult), sem="V", waits=[("A", acp)])

    def in_proj(self, mix):
        kb, T = self.kb, self.T
        kb.phase = "inproj"
        a = mix == "a"
        wsrc = T["W"]["wA" if a else "wB"]
        slot = T["wslotA"]
        xn, z, raw, dtbuf, PSA = T["xnorm"], T["z"], T["raw"], T["dtbuf"], T["PSA"]
        bias = T["cb"]["bias_a"] if a else None
        NCH = 2 if a else 4
        kb.barrier()
        self.phase_gate_p = kb.c["P"]
        self.reset_slots("A")
        zv = rview(z[:], "p (ch dcz l) -> p ch dcz l", ch=NCH, dcz=32)
        xn_cp = kb.c["V"]
        for s in range(66):
            wcp = self.load_strip(wsrc, s, slot[:, (s % 6) * 2048:(s % 6 + 1) * 2048],
                                  ("A", s % 6))
            ep = self.new_epoch()
            for kc in range(16):
                w0 = [("W", wcp), ("V", xn_cp)] if kc == 0 else ()
                self.mm("PSA", PSA[:, (s % 4) * 512:(s % 4 + 1) * 512],
                        slot[:, (s % 6) * 2048 + kc * 128:(s % 6) * 2048 + kc * 128 + 128],
                        xn[:, kc * 512:(kc + 1) * 512], ep,
                        stop=(kc == 15), sem="P" if kc == 15 else None, waits=w0)
            self.slot_free[("A", s % 6)] = ("P", kb.c["P"])
            pcp = kb.c["P"]
            pb = PSA[:, (s % 4) * 512:(s % 4 + 1) * 512]
            if s < 32:
                ov = zv[:, :, s, :]
                iv = rview(pb, "p (ch l) -> p ch l", ch=NCH)
            elif s < 65:
                dcx = s - 32
                ov = raw[:, dcx * 516 + 3:dcx * 516 + 515]
                iv = pb
            else:
                ov = dtbuf[:]
                iv = pb
            if a:
                self.evac("scalar", lambda e, o=ov, i=iv, b=bias[:, s:s + 1]:
                          e.add(out=o, in_=i, add=b), "PSA", pb, "A",
                          waits=[("P", pcp)])
            else:
                self.evac("scalar", lambda e, o=ov, i=iv: e.copy(out=o, in_=i),
                          "PSA", pb, "A", waits=[("P", pcp)])

    def dt_prep(self, mix):
        kb, T = self.kb, self.T
        a = mix == "a"
        dtbuf, dtw1, lndt, afm = T["dtbuf"], T["dtw1"], T["lndt"], T["afm"]
        dtb = T["cb"]["dtb_a" if a else "dtb_b"]
        negA = T["negA_a" if a else "negA_b"]
        acp = kb.c["A"]
        kb.op("vector", lambda e: e.tensor_scalar(
            out=dtw1[:], in0=dtbuf[0:64, :], scalar1=dtb[:, 0:1], scalar2=None,
            op0=ALU.add), sem="V", waits=[("A", acp)])
        kb.op("vector", lambda e: e.tensor_scalar(
            out=afm[:], in0=dtw1[:], scalar1=-1.0, scalar2=None,
            op0=ALU.mult), sem="V")
        kb.op("vector", lambda e: e.tensor_tensor(
            out=lndt[:], in0=dtw1[:], in1=afm[:], op=ALU.max), sem="V")
        vcp = kb.c["V"]
        kb.op("scalar", lambda e: e.activation(
            out=lndt[:], in_=lndt[:], func=AF.Exp, scale=-1.0), sem="A",
            waits=[("V", vcp)])
        bv = T["cb"]["biasvals"]
        kb.op("scalar", lambda e: e.activation(
            out=lndt[:], in_=lndt[:], func=AF.Ln, bias=bv[0:64, 1:2]), sem="A")
        acp2 = kb.c["A"]
        kb.op("vector", lambda e: e.tensor_scalar(
            out=afm[:], in0=dtw1[:], scalar1=0.0, scalar2=None, op0=ALU.max),
            sem="V")
        kb.op("vector", lambda e: e.tensor_tensor(
            out=dtw1[:], in0=afm[:], in1=lndt[:], op=ALU.add), sem="V",
            waits=[("A", acp2)])
        vcp2 = kb.c["V"]
        kb.op("scalar", lambda e: e.activation(
            out=lndt[:], in_=dtw1[:], func=AF.Ln), sem="A", waits=[("V", vcp2)])
        acp3 = kb.c["A"]
        kb.op("vector", lambda e: e.tensor_scalar(
            out=afm[:], in0=dtw1[:], scalar1=negA[:, 0:1], scalar2=None,
            op0=ALU.mult), sem="V", waits=[("A", acp3)])

    def conv_phase(self, mix):
        kb, T = self.kb, self.T
        kb.phase = "conv"
        a = mix == "a"
        raw, conv, tail = T["raw"], T["conv"], T["tail_a" if a else "tail_b"]
        cw = T["cb"]["convw_a" if a else "convw_b"]
        cbias = T["cb"]["convb_a" if a else "convb_b"]
        kb.barrier()
        self.phase_gate_p = kb.c["P"]
        tv = rview(tail[:], "p (dc k) -> p dc k", k=3)
        rv0 = rview(raw[:], "p (dc c) -> p dc c", c=516)
        cvv = rview(conv[:], "p (dc c) -> p dc c", c=512)
        kb.op("vector", lambda e: e.tensor_copy(out=rv0[:, :, 0:3], in_=tv),
              sem="V")
        vtail = kb.c["V"]
        kb.wait("gpsimd", "V", vtail)
        for k in range(4):
            for dc in range(33):
                scl = cw[:, dc * 4 + k:dc * 4 + k + 1]
                ivd = rv0[:, dc, k:k + 512]
                ovd = cvv[:, dc, :]
                eng, sm = ("vector", "V")
                if k == 0:
                    kb.op(eng, lambda e, o=ovd, i=ivd, s=scl: e.tensor_scalar(
                        out=o, in0=i, scalar1=s, scalar2=None, op0=ALU.mult),
                        sem=sm if dc >= 31 else None)
                else:
                    kb.op(eng, lambda e, o=ovd, i=ivd, s=scl: e.scalar_tensor_tensor(
                        out=o, in0=i, scalar=s, in1=o, op0=ALU.mult, op1=ALU.add),
                        sem=sm if dc >= 31 else None)
        kb.op("vector", lambda e: e.tensor_copy(out=tv, in_=rv0[:, :, 512:515]),
              sem="V")
        vcp2 = kb.c["V"]
        gcp2 = kb.c["G"]
        for dc in range(33):
            ov = cvv[:, dc, :]
            kb.op("scalar", lambda e, o=ov, b=cbias[:, dc:dc + 1]: e.activation(
                out=o, in_=o, func=AF.Silu, bias=b),
                sem="A" if dc == 32 else None,
                waits=[("V", vcp2), ("G", gcp2)] if dc == 0 else ())
        self.conv_done_cp = kb.c["A"]

    def ssd_chunk(self, mix, ch):
        kb, T = self.kb, self.T
        kb.phase = "ssdprep_" + mix
        a = mix == "a"
        Q = QA if a else QB
        nh = NHA if a else NHB
        hp = PA if a else PB
        LT = Q // 128
        NG = 4
        GH = nh // NG
        conv, X_tm, Mbuf, yfm, ytm = T["conv"], T["X_tm"], T["Mbuf"], T["yfm"], T["ytm"]
        afm, lndt, cafm, cwork, cwork2 = T["afm"], T["lndt"], T["cafm"], T["cwork"], T["cwork2"]
        a_tm, cs_tm, coeff_tm = T["a_tm"], T["cs_tm"], T["coeff_tm"]
        B_tm, C_sc, E_fm, esrow, esbc = T["B_tm"], T["C_sc"], T["E_fm"], T["esrow"], T["esbc"]
        C0, CBTm, eflat, Ds, ca_bc = T["C0"], T["CBTm"], T["eflat"], T["Ds"], T["ca_bc"]
        xcg, S_ab, dscr, zbuf, zs, dsk = T["xcg"], T["S_ab"], T["dscr"], T["z"], T["zs"], T["dsk"]
        S_st = S_ab if a else T["S_bb"]
        dprm = T["cb"]["dprm"]
        PSA, PSB = T["PSA"], T["PSB"]
        idf, idb = T["cb"]["ident_f"], T["cb"]["ident_bf"]
        onesf, onesb = T["cb"]["ones_f"], T["cb"]["ones_bf"]
        triu = T["cb"]["triu"]
        c0 = ch * Q
        entry_v, entry_a = kb.c["V"], kb.c["A"]
        aConv = self.conv_done_cp

        vA0 = kb.c["V"]
        pA = []
        for lt in range(LT):
            self.mm("PSB", PSB[:, lt * 64:lt * 64 + nh],
                    afm[0:nh, c0 + lt * 128:c0 + lt * 128 + 128],
                    idf[0:nh, 0:nh], self.new_epoch(), sem="P", waits=[("V", vA0)])
        pcp = kb.c["P"]
        for lt in range(LT):
            dst = a_tm[:, lt * 32:lt * 32 + nh] if a else a_tm[:, 0:64]
            self.evac("vector", lambda e, d=dst, s=PSB[:, lt * 64:lt * 64 + nh]:
                      e.tensor_copy(out=d, in_=s), "PSB", PSB[:, lt * 64:lt * 64 + nh],
                      "V", waits=[("P", pcp)])
        vcp = kb.c["V"]
        epc = self.new_epoch()
        for lt in range(LT):
            lhs = a_tm[:, lt * 32:lt * 32 + nh] if a else a_tm[:, 0:64]
            self.mm("PSB", PSB[0:nh, 1024:1024 + Q], lhs,
                    triu[:, lt * 256:lt * 256 + Q], epc, stop=(lt == LT - 1),
                    sem="P" if lt == LT - 1 else None,
                    waits=[("V", vcp)] if lt == 0 else ())
        pcp = kb.c["P"]
        self.evac("vector", lambda e: e.tensor_copy(
            out=cafm[0:nh, 0:Q], in_=PSB[0:nh, 1024:1024 + Q]),
            "PSB", PSB[0:nh, 1024:1024 + Q], "V", waits=[("P", pcp)])
        vca = kb.c["V"]

        if SSD_CUT == 11:
            return
        kb.op("scalar", lambda e: e.activation(out=E_fm[0:nh, 0:Q],
              in_=cafm[0:nh, 0:Q], func=AF.Exp), sem="A", waits=[("V", vca)])
        aE = kb.c["A"]
        dsl = dscr[ch % 2]
        w = [("V", vca)]
        if self.ca_slot_cp.get(ch % 2):
            w.append(("S", self.ca_slot_cp[ch % 2]))
        kb.op("sync", lambda e: e.dma_start(out=dsl[0:nh, 0:Q], in_=cafm[0:nh, 0:Q]),
              sem="S", d=16, waits=w)
        sca = kb.c["S"]
        kb.op("vector", lambda e: e.tensor_tensor(
            out=cwork[0:nh, 0:Q], in0=cafm[0:nh, 0:Q],
            in1=lndt[0:nh, c0:c0 + Q], op=ALU.subtract), sem="V")
        vcs = kb.c["V"]
        for lt in range(LT):
            self.mm("PSB", PSB[:, 512 + lt * 64:512 + lt * 64 + nh],
                    cwork[0:nh, lt * 128:lt * 128 + 128], idf[0:nh, 0:nh],
                    self.new_epoch(), sem="P", waits=[("V", vcs)])
        pcp = kb.c["P"]
        for lt in range(LT):
            src = PSB[:, 512 + lt * 64:512 + lt * 64 + nh]
            self.evac("vector", lambda e, lt=lt, s=src: e.tensor_copy(
                out=cs_tm[:, lt * 64:lt * 64 + nh], in_=s), "PSB", src, "V",
                waits=[("P", pcp)])
        kb.op("vector", lambda e: e.tensor_scalar(
            out=cwork2[0:nh, 0:Q], in0=cafm[0:nh, 0:Q],
            scalar1=cafm[0:nh, Q - 1:Q], scalar2=None, op0=ALU.subtract), sem="V")
        kb.op("vector", lambda e: e.tensor_tensor(
            out=cwork2[0:nh, 0:Q], in0=lndt[0:nh, c0:c0 + Q],
            in1=cwork2[0:nh, 0:Q], op=ALU.subtract), sem="V")
        vco = kb.c["V"]
        kb.op("scalar", lambda e: e.activation(out=cwork2[0:nh, 0:Q],
              in_=cwork2[0:nh, 0:Q], func=AF.Exp), sem="A", waits=[("V", vco)])
        aco = kb.c["A"]
        for lt in range(LT):
            self.mm("PSB", PSB[:, 640 + lt * 64:640 + lt * 64 + nh],
                    cwork2[0:nh, lt * 128:lt * 128 + 128], idf[0:nh, 0:nh],
                    self.new_epoch(), sem="P", waits=[("A", aco)])
        pcp = kb.c["P"]
        for lt in range(LT):
            src = PSB[:, 640 + lt * 64:640 + lt * 64 + nh]
            self.evac("vector", lambda e, lt=lt, s=src: e.tensor_copy(
                out=coeff_tm[:, lt * 64:lt * 64 + nh], in_=s), "PSB", src, "V",
                waits=[("P", pcp)])
        self.mm("PSB", PSB[0:1, 896:896 + nh], cafm[0:nh, Q - 1:Q],
                idf[0:nh, 0:nh], self.new_epoch(), sem="P", waits=[("V", vca)])
        pcp = kb.c["P"]
        self.evac("vector", lambda e: e.tensor_copy(out=esrow[0:1, 0:nh],
                  in_=PSB[0:1, 896:896 + nh]), "PSB", PSB[0:1, 896:896 + nh],
                  "V", waits=[("P", pcp)])
        ves = kb.c["V"]
        kb.op("scalar", lambda e: e.activation(out=esrow[0:1, 0:nh],
              in_=esrow[0:1, 0:nh], func=AF.Exp), sem="A", waits=[("V", ves)])
        aes = kb.c["A"]
        self.mm("PSB", PSB[:, 896:896 + nh], onesf[0:1, 0:128],
                esrow[0:1, 0:nh], self.new_epoch(), sem="P", waits=[("A", aes)])
        pcp = kb.c["P"]
        self.evac("vector", lambda e: e.tensor_copy(out=esbc[0:64, 0:nh],
                  in_=PSB[0:64, 896:896 + nh]), "PSB", PSB[:, 896:896 + nh],
                  "V", waits=[("P", pcp)])
        v_esbc = kb.c["V"]

        if SSD_CUT == 12:
            return
        self.mm("PSB", PSB[0:64, 0:Q], idb[64:128, 64:128],
                conv[64:128, 32 * 512 + c0:32 * 512 + c0 + Q],
                self.new_epoch(), sem="P", waits=[("A", aConv)])
        pcp = kb.c["P"]
        self.evac("vector", lambda e: e.tensor_copy(out=C0[0:64, 0:Q],
                  in_=PSB[0:64, 0:Q]), "PSB", PSB[0:64, 0:Q], "V",
                  waits=[("P", pcp)])
        vC0 = kb.c["V"]
        if SSD_CUT == 15:
            return
        for st in range(LT):
            lhsB = conv[0:64, 32 * 512 + c0 + st * 128:32 * 512 + c0 + st * 128 + 128]
            rhsC = C0[0:64, 0:Q]
            self.mm("PSB", PSB[:, 512 + st * 256:512 + st * 256 + Q],
                    lhsB, rhsC, self.new_epoch(), sem="P",
                    waits=[("V", vC0)])
        pcp = kb.c["P"]
        for st in range(LT):
            src = PSB[:, 512 + st * 256:512 + st * 256 + Q]
            ov = rview(CBTm[:, st * 256:st * 256 + Q], "p (o l) -> p o l", o=1)
            iv = rview(src, "p (o l) -> p o l", o=1)
            tvv = rview(triu[:, st * 256:st * 256 + Q], "p (o l) -> p o l", o=1)
            self.evac("vector", lambda e, ov=ov, iv=iv, tvv=tvv: e.tensor_tensor(
                out=ov, in0=iv, in1=tvv, op=ALU.mult), "PSB", src, "V",
                waits=[("P", pcp)])
        vCBT = kb.c["V"]
        if SSD_CUT == 16:
            return
        for lt in range(LT):
            self.mm("PSB", PSB[:, lt * 64:lt * 64 + 64],
                    conv[0:64, 32 * 512 + c0 + lt * 128:32 * 512 + c0 + lt * 128 + 128],
                    idb[0:64, 0:64], self.new_epoch(), sem="P",
                    waits=[("A", aConv)])
        pcp = kb.c["P"]
        for lt in range(LT):
            src = PSB[:, lt * 64:lt * 64 + 64]
            self.evac("vector", lambda e, lt=lt, s=src: e.tensor_copy(
                out=B_tm[:, lt * 64:lt * 64 + 64], in_=s), "PSB", src, "V",
                waits=[("P", pcp)])
        vBtm = kb.c["V"]
        if SSD_CUT == 13:
            return
        for tt in range(LT):
            for dg in range(8):
                xoff = 1024 + 512 * ((tt * 8 + dg) % 2)
                epx = self.new_epoch()
                for j in range(4):
                    dc = dg * 4 + j
                    self.mm("PSB", PSB[:, xoff + j * 128:xoff + j * 128 + 128],
                            conv[:, dc * 512 + c0 + tt * 128:dc * 512 + c0 + tt * 128 + 128],
                            idb[:], epx, sem="P" if j == 3 else None,
                            waits=[("A", aConv)] if j == 0 else ())
                pcp = kb.c["P"]
                self.evac("scalar", lambda e, tt=tt, dg=dg, xoff=xoff: e.copy(
                    out=X_tm[:, tt * 4096 + dg * 512:tt * 4096 + dg * 512 + 512],
                    in_=PSB[:, xoff:xoff + 512]), "PSB", PSB[:, xoff:xoff + 512], "A",
                    waits=[("P", pcp)])
        aXtm = kb.c["A"]

        if SSD_CUT == 1:
            return
        kb.phase = "ssdY_" + mix
        zofs = ch * 32 * Q
        y_ep = {}
        tt2cp = {}
        g_pcp = {}
        for g in range(NG):
            w = [("A", aE)]
            if g >= 1:
                w.append(("P", g_pcp.get(g - 1, 0)))
            kb.op("sync", lambda e, g=g: e.dma_start(
                out=eflat[0:1, :], in_=E_fm[g * GH:(g + 1) * GH, 0:Q]),
                sem="S", d=16, waits=w)
            s_ef = kb.c["S"]
            seg = dsl[g * GH:(g + 1) * GH, 0:Q]
            seg = seg.rearrange("(o h) l -> o h l", o=1).broadcast_to([128, GH, Q])
            dstv = rview(ca_bc[:, 0:GH * Q], "p (h l) -> p h l", h=GH)
            w = [("S", sca), ("V", entry_v), ("A", entry_a)]
            if g >= 1:
                w.append(("V", tt2cp.get(g - 1, 0)))
            kb.op("sync", lambda e, seg=seg, dstv=dstv: e.dma_start(out=dstv, in_=seg),
                  sem="S", d=16, waits=w)
            s_rep = kb.c["S"]
            self.ca_slot_cp[ch % 2] = kb.c["S"]
            buf = g % 2
            mslc = []
            for st in range(LT):
                dv = rview(Ds[:], "p (h l) -> p h l", h=GH)
                cav = rview(ca_bc[:, 0:GH * Q], "p (h l) -> p h l", h=GH)
                csv = bc3(cs_tm[:, st * 64 + g * GH:st * 64 + (g + 1) * GH], Q)
                w = [("S", s_rep), ("V", vcs + 0), ("A", entry_a)]
                if (buf, st) in self.mfree_cp:
                    w.append(("P", self.mfree_cp[(buf, st)]))
                kb.op("vector", lambda e, dv=dv, cav=cav, csv=csv: e.tensor_tensor(
                    out=dv, in0=cav, in1=csv, op=ALU.subtract), sem="V", waits=w)
                kb.op("vector", lambda e: e.tensor_scalar(
                    out=Ds[:], in0=Ds[:], scalar1=80.0, scalar2=None,
                    op0=ALU.min), sem="V")
                vD = kb.c["V"]
                mofs = buf * 4096 + st * 2048
                msl = Mbuf[:, mofs:mofs + GH * Q]
                kb.op("scalar", lambda e, o=msl: e.activation(
                    out=o, in_=Ds[:, 0:GH * Q], func=AF.Exp), sem="A",
                    waits=[("V", vD)])
                aM = kb.c["A"]
                mv = rview(msl, "p (h l) -> p h l", h=GH)
                cbv = bc_row(CBTm[:, st * 256:st * 256 + Q], GH)
                kb.op("vector", lambda e, mv=mv, cbv=cbv: e.tensor_tensor(
                    out=mv, in0=mv, in1=cbv, op=ALU.mult), sem="V",
                    waits=[("A", aM), ("V", vCBT)])
                mslc.append(msl)
            tt2cp[g] = kb.c["V"]
            if SSD_CUT == 2:
                g_pcp[g] = kb.c["P"]
                for st in range(LT):
                    self.mfree_cp[(buf, st)] = kb.c["V"]
                continue
            epY = [self.new_epoch() for _ in range(LT)]
            C_sc2 = T["C_sc2"]
            GH2 = GH // 2
            vCs_r = {}
            for r in range(2):
                epE = self.new_epoch()
                for i2 in range(GH2):
                    i = r * GH2 + i2
                    self.mm("PSB", PSB[:, 1024 + i2 * Q:1024 + (i2 + 1) * Q],
                            onesb[0:1, 0:128], eflat[0:1, i * Q:(i + 1) * Q], epE,
                            sem="P" if i2 == GH2 - 1 else None,
                            waits=[("S", s_ef)] if i2 == 0 else ())
                pcp = kb.c["P"]
                c0v = bc_row(C0[0:64, 0:Q], GH2)
                epsv = rview(PSB[0:64, 1024:1024 + GH2 * Q], "p (h l) -> p h l", h=GH2)
                csv2 = rview(C_sc2[0:64, r * GH2 * Q:(r + 1) * GH2 * Q],
                             "p (h l) -> p h l", h=GH2)
                self.evac("vector", lambda e, c0v=c0v, epsv=epsv, csv2=csv2:
                          e.tensor_tensor(out=csv2, in0=c0v, in1=epsv, op=ALU.mult),
                          "PSB", PSB[0:64, 1024:1024 + GH2 * Q], "V",
                          waits=[("P", pcp), ("V", vC0)])
                vCs_r[r] = kb.c["V"]
            for i in range(GH):
                h = g * GH + i
                vCs = vCs_r[i // GH2]
                for lt in range(LT):
                    yac = "PSA"
                    ybase = lt * 1024
                    for st in range(LT):
                        w = [("V", tt2cp[g]), ("A", aXtm)] if (st == 0) else ()
                        self.mm(yac, T["PSA"][:, ybase + i * hp:ybase + i * hp + hp],
                                Mbuf[:, buf * 4096 + st * 2048 + i * Q + lt * 128:
                                     buf * 4096 + st * 2048 + i * Q + lt * 128 + 128],
                                X_tm[:, st * 4096 + h * hp:st * 4096 + h * hp + hp],
                                epY[lt], stop=False, waits=w)
                    self.mm(yac, T["PSA"][:, ybase + i * hp:ybase + i * hp + hp],
                            C_sc2[0:64, i * Q + lt * 128:i * Q + lt * 128 + 128],
                            S_st[0:64, h * hp:h * hp + hp], epY[lt], stop=True,
                            sem="P" if (i == GH - 1 and lt == LT - 1) else None,
                            waits=[("V", vCs), ("V", self.s_upd_cp)])
            g_pcp[g] = kb.c["P"]
            for st in range(LT):
                self.mfree_cp[(buf, st)] = g_pcp[g]
            for lt in range(LT):
                src = T["PSA"][:, lt * 1024:lt * 1024 + GH * hp]
                w = [("P", g_pcp[g])]
                if self.ytm_free:
                    w.append(("P", self.ytm_free))
                aY = self.evac("scalar", lambda e, s=src: e.copy(
                    out=ytm[:, 0:GH * hp], in_=s), "PSA", src, "A", waits=w)
                ept = self.new_epoch()
                nslot = 1024 // hp
                for i in range(GH):
                    h = g * GH + i
                    if a:
                        dst = PSB[:, (i % 8) * 128:(i % 8) * 128 + 128]
                    else:
                        base = 64 * (h % 2)
                        dst = PSB[base:base + 64, (i // 2 % 8) * 128:(i // 2 % 8) * 128 + 128]
                    if i % 16 == 0:
                        ept = self.new_epoch()
                    self.mm("PSB", dst, ytm[:, i * hp:i * hp + hp], idb[:],
                            ept, sem="P" if (i % 16 == 15 or i == GH - 1) else None,
                            waits=[("A", aY)] if i == 0 else ())
                    if (i % 16 == 15 or i == GH - 1):
                        pcp = kb.c["P"]
                        self.ytm_free = pcp
                        nle = (i % 16) + 1
                        if a:
                            ysl = rview(yfm[:, 0:8192], "p (dc l) -> p dc l", l=Q)[
                                :, g * GH:g * GH + 8, lt * 128:lt * 128 + 128]
                            src2 = rview(PSB[:, 0:1024], "p (i l) -> p i l", i=8)
                        else:
                            ysl = rview(yfm[:, 0:4096], "p (dc l) -> p dc l", l=128)[
                                :, g * 8:g * 8 + 8, :]
                            src2 = rview(PSB[:, 0:1024], "p (i l) -> p i l", i=8)
                        self.evac("scalar", lambda e, o=ysl, s=src2: e.copy(
                            out=o, in_=s), "PSB", PSB[:, 0:1024], "A",
                            waits=[("P", pcp)])
        ayf = kb.c["A"]
        if SSD_CUT in (2, 3):
            return

        kb.phase = "ssdstates_" + mix
        for half in range(2):
            hh = nh // 2
            ep = self.new_epoch()
            for lt in range(LT):
                cof = bc3(coeff_tm[:, lt * 64 + half * hh:lt * 64 + half * hh + hh], hp)
                xs = rview(X_tm[:, lt * 4096 + half * 2048:lt * 4096 + half * 2048 + 2048],
                           "p (h q) -> p h q", h=hh)
                xg = rview(xcg[:, lt * 2048:lt * 2048 + 2048], "p (h q) -> p h q", h=hh)
                w = [("A", aXtm)]
                if half == 0 and lt == 0:
                    w.append(("P", g_pcp[NG - 1]))
                kb.op("vector", lambda e, xg=xg, xs=xs, cof=cof: e.tensor_tensor(
                    out=xg, in0=xs, in1=cof, op=ALU.mult), sem="V", waits=w)
            vxg = kb.c["V"]
            for lt in range(LT):
                for j in range(4):
                    w = [("V", vxg)]
                    self.mm("PSA", PSA[0:64, j * 512:j * 512 + 512],
                            B_tm[:, lt * 64:lt * 64 + 64],
                            xcg[:, lt * 2048 + j * 512:lt * 2048 + j * 512 + 512],
                            ep, stop=(lt == LT - 1),
                            sem="P" if (lt == LT - 1 and j == 3) else None,
                            waits=w if (lt == 0 and j == 0) else [("V", vBtm)] if False else ())
            pst = kb.c["P"]
            sofs = half * 2048
            ssl = S_st[0:64, sofs:sofs + 2048]
            sv = rview(ssl, "p (h q) -> p h q", h=hh)
            ev = bc3(esbc[0:64, half * hh:half * hh + hh], hp)
            kb.op("vector", lambda e, sv=sv, ev=ev: e.tensor_tensor(
                out=sv, in0=sv, in1=ev, op=ALU.mult), sem="V",
                waits=[("P", g_pcp[NG - 1]), ("V", v_esbc)])
            psl = PSA[0:64, 0:2048]
            self.evac("vector", lambda e, ssl=ssl, psl=psl: e.tensor_tensor(
                out=ssl, in0=ssl, in1=psl, op=ALU.add),
                "PSA", psl, "V", waits=[("P", pst)])
        self.s_upd_cp = kb.c["V"]

        if SSD_CUT == 4:
            return
        if not a:
            xv = rview(conv[:], "p (dc c) -> p dc c", c=512)[:, 0:32, c0:c0 + Q]
            dpv = bc3(dprm[:, 0:32], Q)
            kb.op("vector", lambda e, xv=xv, dpv=dpv: e.tensor_tensor(
                out=dsk[:, 0:4096].rearrange("p (dc l) -> p dc l", dc=32),
                in0=xv, in1=dpv, op=ALU.mult), sem="V", waits=[("A", ayf)])
            kb.op("vector", lambda e: e.tensor_tensor(
                out=yfm[:, 0:4096], in0=yfm[:, 0:4096], in1=dsk[:, 0:4096],
                op=ALU.add), sem="V")
        vdsk = kb.c["V"]

        nz = 32 * Q
        zsl = zbuf[:, zofs:zofs + nz]
        kb.op("scalar", lambda e, zsl=zsl: e.activation(
            out=zs[:, 0:nz], in_=zsl, func=AF.Silu), sem="A",
            waits=[("A", ayf), ("V", vdsk)])
        acp = kb.c["A"]
        kb.op("vector", lambda e, zsl=zsl: e.tensor_tensor(
            out=zsl, in0=yfm[:, 0:nz], in1=zs[:, 0:nz], op=ALU.mult),
            sem="V", waits=[("A", acp)])

    def norm_out(self, mix):
        kb, T = self.kb, self.T
        kb.phase = "normout"
        a = mix == "a"
        Q = QA if a else QB
        NCH = 2 if a else 4
        z, sqb, stats, x_sb = T["z"], T["sqb"], T["stats"], T["x_sb"]
        PSA, PSB = T["PSA"], T["PSB"]
        onescol = T["cb"]["onescol_bf"]
        wsrc = T["W"]["wOa" if a else "wOb"]
        slot = T["wslotO"]
        obias = T["cb"]["obias_a"] if a else None
        lnw = T["cb"]["lnw_a" if a else "normw_b"]
        lnb = T["cb"]["lnb_a"] if a else None
        kb.barrier()
        self.phase_gate_p = kb.c["P"]
        self.reset_slots("O")
        vz = kb.c["V"]
        zv = rview(z[:], "p (ch dcz l) -> p ch dcz l", ch=NCH, dcz=32)
        ep = self.new_epoch()
        for ch in range(NCH):
            for dc in range(32):
                last = dc == 31
                self.mm("PSB", PSB[0:1, ch * Q:(ch + 1) * Q], onescol[:],
                        zv[:, ch, dc, :], ep, stop=last,
                        sem="P" if (last and ch == NCH - 1) else None,
                        waits=[("V", vz)] if (dc == 0 and ch == 0) else ())
        kb.op("scalar", lambda e: e.activation(
            out=sqb[:], in_=z[:], func=AF.Square), sem="A", waits=[("V", vz)])
        acp = kb.c["A"]
        sqv = rview(sqb[:], "p (ch dcz l) -> p ch dcz l", ch=NCH, dcz=32)
        ep2 = self.new_epoch()
        for ch in range(NCH):
            for dc in range(32):
                last = dc == 31
                self.mm("PSB", PSB[0:1, 512 + ch * Q:512 + (ch + 1) * Q],
                        onescol[:], sqv[:, ch, dc, :], ep2, stop=last,
                        sem="P" if (last and ch == NCH - 1) else None,
                        waits=[("A", acp)] if (dc == 0 and ch == 0) else ())
        pcp = kb.c["P"]
        self._stats(pcp, sub_mean=a, n=4096.0)
        mv = bc_mid = None
        mub, rsb = T["mubc"], T["rsbc"]
        zn = rview(z[:], "p (ch dcz l) -> p ch dcz l", ch=NCH, dcz=32)
        muv = rview(mub[:, 0:512], "p (ch l) -> p ch l", ch=NCH)
        muv = muv.rearrange("p ch (o l) -> p ch o l", o=1).broadcast_to([128, NCH, 32, Q])
        rsv = rview(rsb[:, 0:512], "p (ch l) -> p ch l", ch=NCH)
        rsv = rsv.rearrange("p ch (o l) -> p ch o l", o=1).broadcast_to([128, NCH, 32, Q])
        self._normalize(None, 0, 0, sub_mean=a, zview=(zn, muv, rsv))
        vn = kb.c["V"]
        for dc in range(32):
            ov = zv[:, :, dc, :]
            if a:
                kb.op("vector", lambda e, o=ov, w1=lnw[:, dc:dc + 1], w2=lnb[:, dc:dc + 1]:
                      e.tensor_scalar(out=o, in0=o, scalar1=w1, scalar2=w2,
                                      op0=ALU.mult, op1=ALU.add),
                      sem="V" if dc == 31 else None)
            else:
                kb.op("vector", lambda e, o=ov, w1=lnw[:, dc:dc + 1]:
                      e.tensor_scalar(out=o, in0=o, scalar1=w1, scalar2=None,
                                      op0=ALU.mult),
                      sem="V" if dc == 31 else None)
        vaff = kb.c["V"]
        kb.phase = "outproj"
        for s in range(16):
            wcp = self.load_strip(wsrc, s, slot[:, (s % 2) * 4096:(s % 2 + 1) * 4096],
                                  ("O", s % 2))
            ep = self.new_epoch()
            first = True
            for kc in range(32):
                for ch in range(NCH):
                    w0 = [("W", wcp), ("V", vaff)] if first else ()
                    first = False
                    self.mm("PSA", PSA[:, (s % 4) * 512 + ch * Q:(s % 4) * 512 + (ch + 1) * Q],
                            slot[:, (s % 2) * 4096 + kc * 128:(s % 2) * 4096 + kc * 128 + 128],
                            zv[:, ch, kc, :], ep,
                            stop=(kc == 31), sem="P" if (kc == 31 and ch == NCH - 1) else None,
                            waits=w0)
            self.slot_free[("O", s % 2)] = ("P", kb.c["P"])
            pcp = kb.c["P"]
            pb = PSA[:, (s % 4) * 512:(s % 4 + 1) * 512]
            xo = x_sb[:, s * 512:(s + 1) * 512]
            if a:
                self.evac("vector", lambda e, o=xo, i=pb, b=obias[:, s:s + 1]:
                          e.scalar_tensor_tensor(out=o, in0=i, scalar=b, in1=o,
                                                 op0=ALU.add, op1=ALU.add),
                          "PSA", pb, "V", waits=[("P", pcp)])
            else:
                self.evac("vector", lambda e, o=xo, i=pb: e.tensor_tensor(
                    out=o, in0=o, in1=i, op=ALU.add), "PSA", pb, "V",
                    waits=[("P", pcp)])

    def ffn(self):
        kb, T = self.kb, self.T
        kb.phase = "ffn"
        xn, hsp, ffs = T["xnorm2"], T["hsp_fm"], T["ffs"]
        slotG, slotU, slotD = T["wslotG"], T["wslotU"], T["wslotD"]
        PSA, PSB = T["PSA"], T["PSB"]
        idb = T["cb"]["ident_bf"]
        x_sb = T["x_sb"]
        kb.barrier()
        self.phase_gate_p = kb.c["P"]
        self.reset_slots("G", "U", "D")
        vxn = kb.c["V"]
        sg = ffs[:, 0:1024]
        hb = ffs[:, 1024:2048]
        ha = ffs[:, 2048:3072]
        t2 = ffs[:, 3072:3328]
        msk = ffs[:, 3328:4352]
        hsts = [ffs[:, 4352:5376], T["hst2"][:, 0:1024]]
        vhs_log = {}
        tr_done = {}

        def emit_tr(s_):
            hst = hsts[s_ % 2]
            for ts_ in range(4):
                ept = self.new_epoch()
                toff = 512 * (ts_ % 2)
                for j in range(2):
                    self.mm("PSA", PSA[:, toff + j * 128:toff + j * 128 + 128],
                            hst[:, ts_ * 256 + j * 128:ts_ * 256 + j * 128 + 128],
                            idb[:], ept, sem="P" if j == 1 else None,
                            waits=[("V", vhs_log[s_])] if (ts_ == 0 and j == 0) else ())
                pcp = kb.c["P"]
                ic0 = 2 * s_
                ov = rview(hsp[:, 0:64 * 512], "p (ic t) -> p ic t", t=512)[
                    :, ic0:ic0 + 2, ts_ * 128:ts_ * 128 + 128]
                self.evac("scalar", lambda e, ov=ov, toff=toff: e.copy(
                    out=ov, in_=rview(PSA[:, toff:toff + 256], "p (ic t) -> p ic t", t=128)),
                    "PSA", PSA[:, toff:toff + 256], "A", waits=[("P", pcp)])
            tr_done[s_] = kb.c["A"]

        for s_ in range(32):
            wg = self.load_strip(T["W"]["wG"], s_, slotG[:, (s_ % 2) * 4096:(s_ % 2 + 1) * 4096],
                                 ("G", s_ % 2))
            wu = self.load_strip(T["W"]["wU"], s_, slotU[:, (s_ % 2) * 4096:(s_ % 2 + 1) * 4096],
                                 ("U", s_ % 2))
            epg, epu = self.new_epoch(), self.new_epoch()
            gcp, ucp = {}, {}
            for ts_ in range(4):
                for kc in range(16):
                    w0 = [("W", wg), ("V", vxn)] if kc == 0 else ()
                    self.mm("PSB", PSB[:, ts_ * 256:(ts_ + 1) * 256],
                            xn[:, kc * 512 + ts_ * 128:kc * 512 + ts_ * 128 + 128],
                            slotG[:, (s_ % 2) * 4096 + kc * 256:(s_ % 2) * 4096 + (kc + 1) * 256],
                            epg, stop=(kc == 15), sem="P" if kc == 15 else None, waits=w0)
                gcp[ts_] = kb.c["P"]
            for ts_ in range(4):
                for kc in range(16):
                    w0 = [("W", wu)] if kc == 0 else ()
                    self.mm("PSB", PSB[:, 1024 + ts_ * 256:1024 + (ts_ + 1) * 256],
                            xn[:, kc * 512 + ts_ * 128:kc * 512 + ts_ * 128 + 128],
                            slotU[:, (s_ % 2) * 4096 + kc * 256:(s_ % 2) * 4096 + (kc + 1) * 256],
                            epu, stop=(kc == 15), sem="P" if kc == 15 else None, waits=w0)
                ucp[ts_] = kb.c["P"]
            self.slot_free[("G", s_ % 2)] = ("P", kb.c["P"])
            self.slot_free[("U", s_ % 2)] = ("P", kb.c["P"])
            if s_ >= 1:
                emit_tr(s_ - 1)
            for pr_ in range(2):
                psl = PSB[:, pr_ * 512:(pr_ + 1) * 512]
                self.evac("scalar", lambda e, o=sg[:, pr_ * 512:(pr_ + 1) * 512], i=psl:
                          e.activation(out=o, in_=i, func=AF.Silu), "PSB", psl,
                          "A", waits=[("P", gcp[2 * pr_ + 1])])
                acp = kb.c["A"]
                pul = PSB[:, 1024 + pr_ * 512:1024 + (pr_ + 1) * 512]
                self.evac("vector", lambda e, o=hb[:, pr_ * 512:(pr_ + 1) * 512],
                          i0=sg[:, pr_ * 512:(pr_ + 1) * 512], i1=pul:
                          e.tensor_tensor(out=o, in0=i0, in1=i1, op=ALU.mult),
                          "PSB", pul, "V", waits=[("P", ucp[2 * pr_ + 1]), ("A", acp)])
            vH = kb.c["V"]
            kb.op("scalar", lambda e: e.activation(
                out=ha[:], in_=hb[:], func=AF.Square), sem="A", waits=[("V", vH)])
            aAb = kb.c["A"]
            hv = rview(ha[:], "p (gq four) -> p gq four", four=4)
            kb.op("vector", lambda e: e.tensor_tensor(
                out=t2[:, 0:256].rearrange("p (g o) -> p g o", o=1),
                in0=hv[:, :, 0:1], in1=hv[:, :, 1:2], op=ALU.max), sem="V",
                waits=[("A", aAb)])
            kb.op("vector", lambda e: e.tensor_tensor(
                out=msk[:, 0:256].rearrange("p (g o) -> p g o", o=1),
                in0=hv[:, :, 0:1], in1=hv[:, :, 1:2], op=ALU.min), sem="V")
            kb.op("vector", lambda e: e.tensor_tensor(
                out=msk[:, 256:512].rearrange("p (g o) -> p g o", o=1),
                in0=hv[:, :, 2:3], in1=hv[:, :, 3:4], op=ALU.max), sem="V")
            kb.op("vector", lambda e: e.tensor_tensor(
                out=msk[:, 512:768].rearrange("p (g o) -> p g o", o=1),
                in0=hv[:, :, 2:3], in1=hv[:, :, 3:4], op=ALU.min), sem="V")
            kb.op("vector", lambda e: e.tensor_tensor(
                out=t2[:, 0:256], in0=t2[:, 0:256], in1=msk[:, 256:512],
                op=ALU.min), sem="V")
            kb.op("vector", lambda e: e.tensor_tensor(
                out=msk[:, 0:256], in0=msk[:, 0:256], in1=msk[:, 512:768],
                op=ALU.max), sem="V")
            kb.op("vector", lambda e: e.tensor_tensor(
                out=t2[:, 0:256], in0=t2[:, 0:256], in1=msk[:, 0:256],
                op=ALU.max), sem="V")
            t2r = rview(t2[:, 0:256], "p (g o) -> p g o", o=1).broadcast_to([128, 256, 4])
            kb.op("vector", lambda e, t2r=t2r: e.tensor_tensor(
                out=rview(msk[:], "p (g four) -> p g four", four=4),
                in0=hv, in1=t2r, op=ALU.is_ge), sem="V")
            hst = hsts[s_ % 2]
            w_hst = [("A", tr_done[s_ - 2])] if s_ >= 2 else []
            kb.op("vector", lambda e, hst=hst: e.tensor_tensor(
                out=hst[:], in0=hb[:], in1=msk[:], op=ALU.mult), sem="V",
                waits=w_hst)
            vhs_log[s_] = kb.c["V"]
        emit_tr(31)
        ahsp = kb.c["A"]
        kb.phase = "ffndown"
        self.phase_gate_p = kb.c["P"]
        self.reset_slots("D")
        for dt_ in range(16):
            wd = self.load_strip(T["W"]["wD"], dt_, slotD[:, (dt_ % 2) * 8192:(dt_ % 2 + 1) * 8192],
                                 ("D", dt_ % 2))
            ep = self.new_epoch()
            for ic in range(64):
                w0 = [("W", wd), ("A", ahsp)] if ic == 0 else ()
                self.mm("PSA", PSA[:, (dt_ % 2) * 512 + 1024:(dt_ % 2) * 512 + 1536],
                        slotD[:, (dt_ % 2) * 8192 + ic * 128:(dt_ % 2) * 8192 + ic * 128 + 128],
                        hsp[:, ic * 512:(ic + 1) * 512], ep,
                        stop=(ic == 63), sem="P" if ic == 63 else None, waits=w0)
            self.slot_free[("D", dt_ % 2)] = ("P", kb.c["P"])
            pcp = kb.c["P"]
            pb = PSA[:, (dt_ % 2) * 512 + 1024:(dt_ % 2) * 512 + 1536]
            xo = T["x_sb"][:, dt_ * 512:(dt_ + 1) * 512]
            self.evac("vector", lambda e, o=xo, i=pb: e.tensor_tensor(
                out=o, in0=o, in1=i, op=ALU.add), "PSA", pb, "V",
                waits=[("P", pcp)])

    def load_x(self, tile):
        kb, T = self.kb, self.T
        u, x_sb = T["u"], T["x_sb"]
        kb.barrier(engs=("gpsimd",))
        src = rview(u, "(dc p) t -> p dc t", p=128)[:, :, tile * 512:(tile + 1) * 512]
        kb.op("gpsimd", lambda e: e.dma_start(
            out=rview(x_sb[:], "p (dc t) -> p dc t", t=512), in_=src),
            sem="W", d=16, waits=[("O", kb.c["O"])])
        scp = kb.c["W"]
        for e in ("tensor", "vector", "scalar", "sync"):
            kb.wait(e, "W", scp)

    def store_x(self, tile):
        kb, T = self.kb, self.T
        out, x_sb = T["out"], T["x_sb"]
        kb.barrier(engs=("gpsimd",))
        dst = rview(out, "(dc p) t -> p dc t", p=128)[:, :, tile * 512:(tile + 1) * 512]
        kb.op("gpsimd", lambda e: e.dma_start(
            out=dst, in_=rview(x_sb[:], "p (dc t) -> p dc t", t=512)),
            sem="O", d=16)

    def finish(self):
        kb = self.kb
        kb.wait("sync", "O", kb.c["O"])
        kb.wait("gpsimd", "O", kb.c["O"])



def build_program(upto="full"):
    nc, kb = build_core(upto)
    pr = Prog(nc, kb)
    T = kb.T
    pr.preamble()
    stages = {"ln1": 0.5, "res1": 1, "res2": 2, "res3": 3, "full": 4,
              "only_b": 10, "only_ffn": 11}
    lim = stages[upto]
    if lim == 10:
        for tile in range(NTILE):
            pr.load_x(tile)
            pr.ln_x(T["xnorm"])
            pr.in_proj("b")
            if SSD_CUT != 5:
                pr.dt_prep("b")
                if SSD_CUT != 6:
                    pr.conv_phase("b")
                    if SSD_CUT != 7:
                        for ch in range(4):
                            pr.ssd_chunk("b", ch)
                        if SSD_CUT == 0:
                            pr.norm_out("b")
            kb.barrier(engs=("sync",))
            pr.store_x(tile)
        pr.finish()
        kb.check_waits()
        with nc.Block() as block:
            kb.finalize_into_block(block)
        return nc
    if lim == 11:
        for tile in range(NTILE):
            pr.load_x(tile)
            pr.ln_x(T["xnorm2"])
            pr.ffn()
            kb.barrier(engs=("sync",))
            pr.store_x(tile)
        pr.finish()
        kb.check_waits()
        with nc.Block() as block:
            kb.finalize_into_block(block)
        return nc
    for tile in range(NTILE):
        pr.load_x(tile)
        pr.ln_x(T["xnorm"])
        if lim == 0.5:
            vcp = kb.c["V"]
            kb.op("vector", lambda e: e.tensor_copy(out=T["x_sb"][:],
                  in_=T["xnorm"][:]), sem="V", waits=[("V", vcp), ("O", kb.c["O"])])
            kb.barrier(engs=("sync",))
            pr.store_x(tile)
            continue
        pr.in_proj("a")
        pr.dt_prep("a")
        pr.conv_phase("a")
        for ch in range(2):
            pr.ssd_chunk("a", ch)
        pr.norm_out("a")
        if lim == 1:
            kb.barrier(engs=("sync",))
            pr.store_x(tile)
            continue
        pr.ln_x(T["xnorm2"])
        pr.ffn()
        if lim == 2:
            kb.barrier(engs=("sync",))
            pr.store_x(tile)
            continue
        pr.ln_x(T["xnorm"])
        pr.in_proj("b")
        pr.dt_prep("b")
        pr.conv_phase("b")
        for ch in range(4):
            pr.ssd_chunk("b", ch)
        pr.norm_out("b")
        if lim == 3:
            kb.barrier(engs=("sync",))
            pr.store_x(tile)
            continue
        pr.ln_x(T["xnorm2"])
        pr.ffn()
        kb.barrier(engs=("sync",))
        pr.store_x(tile)
    pr.finish()
    kb.check_waits()
    PE_LABELS.append(list(kb.pe_labels))
    with nc.Block() as block:
        kb.finalize_into_block(block)
    return nc



def dump_pe_labels(upto="full"):
    import json
    nc, kb = None, None
    nc = build_program(upto)
    return PE_LABELS[-1]


PE_LABELS = []


def kernel(**inputs):
    upto = inputs.pop("_upto", "full")
    ncores = 8
    u = np.asarray(inputs["L0_attn_a_u"], dtype=np.float32)
    st = stage_weights(inputs)
    nc = build_program(upto)
    in_maps = []
    for c in range(ncores):
        m = dict(st)
        m["u"] = np.ascontiguousarray(u[c].T)
        in_maps.append(m)
    res = run_bass_kernel_spmd(nc, in_maps, core_ids=list(range(ncores)))
    outs = [np.ascontiguousarray(r["out"].T) for r in res.results]
    return np.stack(outs, axis=0)


# version 11
# speedup vs baseline: 1.0158x; 1.0158x over previous
import numpy as np
import ml_dtypes
import concourse.bass as bass
import concourse.mybir as mybir
from concourse.bass_utils import run_bass_kernel_spmd

F32 = mybir.dt.float32
BF16 = mybir.dt.bfloat16
AF = mybir.ActivationFunctionType
ALU = mybir.AluOpType
BF = ml_dtypes.bfloat16

T = 512
NTILE = 4
D = 2048
QA, QB = 256, 128
NHA, NHB = 32, 64
PA, PB = 128, 64
NST = 64

SBUF_LIMIT = 229247
SSD_CUT = 0


def _lhsT_strips(wT):
    K, N = wT.shape
    kc, ns = K // 128, N // 128
    A = wT.reshape(kc, 128, ns, 128)
    return np.ascontiguousarray(A.transpose(2, 1, 0, 3)).reshape(ns, 128, K)


def _rhs_strips(w, sw):
    K, N = w.shape
    kc, ns = K // 128, N // sw
    A = w.reshape(kc, 128, ns, sw)
    return np.ascontiguousarray(A.transpose(2, 1, 0, 3)).reshape(ns, 128, kc * sw)


def _col128(v):
    return np.ascontiguousarray(v.reshape(-1, 128).T)


def stage_weights(ins):
    st = {}
    f32 = np.float32

    wa = np.pad(ins["L0_attn_a_in_proj_w"].astype(f32), ((0, 96), (0, 0)))
    st["wA"] = _lhsT_strips(np.ascontiguousarray(wa.T))
    wb = np.pad(ins["L0_attn_b_W_in"].astype(f32), ((0, 64), (0, 0)))
    st["wB"] = _lhsT_strips(np.ascontiguousarray(wb.T))
    st["wOa"] = _lhsT_strips(np.ascontiguousarray(ins["L0_attn_a_out_proj_w"].astype(f32).T))
    st["wOb"] = _lhsT_strips(np.ascontiguousarray(ins["L0_attn_b_W_out"].astype(f32).T))
    st["wG"] = _rhs_strips(ins["L0_ffn_w_gate"].astype(f32), 256)
    st["wU"] = _rhs_strips(ins["L0_ffn_w_up"].astype(f32), 256)
    st["wD"] = _lhsT_strips(ins["L0_ffn_w_down"].astype(f32))

    st["bias_a"] = _col128(np.pad(ins["L0_attn_a_in_proj_b"].astype(f32), (0, 96)))
    st["obias_a"] = _col128(ins["L0_attn_a_out_proj_b"].astype(f32))
    cwa = ins["L0_attn_a_conv_w"].astype(f32).reshape(4224, 4)
    st["convw_a"] = np.ascontiguousarray(cwa.reshape(33, 128, 4).transpose(1, 0, 2)).reshape(128, 132)
    st["convb_a"] = _col128(ins["L0_attn_a_conv_b"].astype(f32))
    cwb = ins["L0_attn_b_conv_w"].astype(f32)
    st["convw_b"] = np.ascontiguousarray(cwb.reshape(33, 128, 4).transpose(1, 0, 2)).reshape(128, 132)
    st["convb_b"] = _col128(ins["L0_attn_b_conv_b"].astype(f32))
    st["lnw_a"] = _col128(ins["L0_attn_a_ln_w"].astype(f32))
    st["lnb_a"] = _col128(ins["L0_attn_a_ln_b"].astype(f32))
    st["normw_b"] = _col128(ins["L0_attn_b_norm_w"].astype(f32))
    st["dprm"] = _col128(np.repeat(ins["L0_attn_b_D_param"].astype(f32), PB))
    st["dtb_a"] = np.pad(ins["L0_attn_a_dt_bias"].astype(f32), (0, 32)).reshape(64, 1)
    st["alog_a"] = np.pad(ins["L0_attn_a_A_log"].astype(f32), (0, 32)).reshape(64, 1)
    st["dtb_b"] = ins["L0_attn_b_dt_bias"].astype(f32).reshape(64, 1)
    st["alog_b"] = ins["L0_attn_b_A_log"].astype(f32).reshape(64, 1)

    st["triu"] = np.zeros((128, 512), f32)
    for stile in range(2):
        for p in range(128):
            for_l = np.arange(256)
            st["triu"][p, stile * 256:(stile + 1) * 256] = (stile * 128 + p <= for_l).astype(f32)
    st["ident_bf"] = np.eye(128, dtype=BF)
    st["ident_f"] = np.eye(64, dtype=f32)
    st["ones_bf"] = np.ones((1, 128), dtype=BF)
    st["ones_f"] = np.ones((1, 128), dtype=f32)
    st["onescol_bf"] = np.ones((128, 1), dtype=BF)
    st["biasvals"] = np.tile(np.array([[1e-5, 1.0]], dtype=f32), (128, 1))
    return st



class KB:

    ENGS = ("sync", "gpsimd", "tensor", "vector", "scalar")

    def __init__(self, nc, upto="full"):
        self.nc = nc
        self.upto = upto
        self.streams = {e: [] for e in self.ENGS}
        self.c = {"W": 0, "S": 0, "P": 0, "V": 0, "A": 0, "O": 0, "G": 0}
        self.sem = {}
        self.lastw = {e: {} for e in self.ENGS}
        self.sbuf_ptr = 0
        self.tensors = {}
        self.wait_log = []
        self.phase = "pre"
        self.pe_labels = []

    def alloc(self, name, shape, dtype, nbytes=None):
        t = self.nc.alloc_sbuf_tensor_at(name, list(shape), dtype, offset=self.sbuf_ptr)
        if nbytes is None:
            esz = 2 if dtype == BF16 else 4
            nbytes = int(np.prod(shape[1:])) * esz
        self.sbuf_ptr += (nbytes + 31) // 32 * 32
        assert self.sbuf_ptr <= SBUF_LIMIT, f"SBUF overflow at {name}: {self.sbuf_ptr}"
        self.tensors[name] = t
        return t

    def alloc_at(self, name, shape, dtype, offset):
        t = self.nc.alloc_sbuf_tensor_at(name, list(shape), dtype, offset=offset)
        self.tensors[name] = t
        return t

    def emit(self, eng, fn):
        self.streams[eng].append(fn)

    def wait(self, eng, sem, val):
        if val <= 0:
            return
        if self.lastw[eng].get(sem, -1) >= val:
            return
        self.lastw[eng][sem] = val
        self.wait_log.append((eng, sem, val))
        h = self.sem[sem]
        self.emit(eng, lambda e, h=h, v=val: e.wait_ge(h, v))

    def bump(self, sem, d):
        self.c[sem] += d
        return self.c[sem]

    def run(self, inst, sem=None, d=1):
        if sem is None:
            return inst
        h = self.sem[sem]
        self.bump(sem, d)

        def g(e, inst=inst, h=h, d=d):
            inst(e).then_inc(h, d)
        return g

    def op(self, eng, inst, sem=None, d=1, waits=()):
        for (s, v) in waits:
            self.wait(eng, s, v)
        self.emit(eng, self.run(inst, sem, d))

    def barrier(self, engs=("tensor", "vector", "scalar", "sync")):
        snap = [("P", self.c["P"]), ("V", self.c["V"]), ("A", self.c["A"]),
                ("S", self.c["S"])]
        for e in engs:
            for s, v in snap:
                self.wait(e, s, v)

    def check_waits(self):
        bad = [(e, s, v, self.c[s]) for (e, s, v) in self.wait_log if v > self.c[s]]
        assert not bad, f"waits exceeding final counts (hang): {bad[:10]}"

    def finalize_into_block(self, block):
        for name, dec in (("sync", block.sync), ("gpsimd", block.gpsimd),
                          ("tensor", block.tensor), ("vector", block.vector),
                          ("scalar", block.scalar)):
            fns = self.streams[name]

            def make(fns):
                def _(eng):
                    for f in fns:
                        f(eng)
                return _
            dec(make(fns))


BASE = 16512
OFF_X = BASE + 0
OFF_S = BASE + 32768
OFF_HSP = BASE + 49152
OFF_ZB = BASE + 114688
OFF_FB = BASE + 155648
OFF_SCRAP = BASE + 172032
OFF_CONST = BASE + 188416


def build_core(upto="full"):
    nc = bass.Bass("TRN2", target_bir_lowering=False, detect_race_conditions=False)
    kb = KB(nc, upto)

    u = nc.dram_tensor("u", [2048, 2048], F32, kind="ExternalInput").ap()
    out = nc.dram_tensor("out", [2048, 2048], F32, kind="ExternalOutput").ap()
    W = {}
    for nm, shp in [("wA", [66, 128, 2048]), ("wB", [66, 128, 2048]),
                    ("wOa", [16, 128, 4096]), ("wOb", [16, 128, 4096]),
                    ("wG", [32, 128, 4096]), ("wU", [32, 128, 4096]),
                    ("wD", [16, 128, 8192])]:
        W[nm] = nc.dram_tensor(nm, shp, F32, kind="ExternalInput").ap()
    P = {}
    for nm, shp, dt in [("bias_a", [128, 66], F32), ("obias_a", [128, 16], F32),
                        ("convw_a", [128, 132], F32), ("convb_a", [128, 33], F32),
                        ("convw_b", [128, 132], F32), ("convb_b", [128, 33], F32),
                        ("lnw_a", [128, 32], F32), ("lnb_a", [128, 32], F32),
                        ("normw_b", [128, 32], F32), ("dprm", [128, 32], F32),
                        ("dtb_a", [64, 1], F32), ("alog_a", [64, 1], F32),
                        ("dtb_b", [64, 1], F32), ("alog_b", [64, 1], F32),
                        ("triu", [128, 512], F32), ("ident_bf", [128, 128], BF16),
                        ("ident_f", [64, 64], F32), ("ones_bf", [1, 128], BF16),
                        ("ones_f", [1, 128], F32), ("onescol_bf", [128, 1], BF16),
                        ("biasvals", [128, 2], F32)]:
        P[nm] = nc.dram_tensor(nm, shp, dt, kind="ExternalInput").ap()
    dscr = nc.dram_tensor("dscr", [2, 64, 256], F32, kind="Internal").ap()

    A = kb.alloc_at
    x_sb = A("x_sb", [128, 8192], BF16, OFF_X)
    S_bb = A("S_bb", [64, 4096], BF16, OFF_X + 16384)
    C_sc2 = A("C_sc2", [64, 2048], BF16, OFF_X + 24576)
    hst2 = A("hst2", [128, 1024], BF16, OFF_FB + 12288)
    S_ab = A("S_ab", [128, 4096], BF16, OFF_S)
    raw = A("raw", [128, 33 * 516], BF16, OFF_HSP)
    wslotA = A("wslotA", [128, 4 * 2048], BF16, OFF_HSP + 34816)
    conv = A("conv", [128, 33 * 512], BF16, OFF_HSP + 34080)
    X_tm = A("X_tm", [128, 2 * 4096], BF16, OFF_HSP)
    Mbuf = A("Mbuf", [128, 2 * 4096], BF16, OFF_HSP + 16384)
    wslotO = A("wslotO", [128, 2 * 4096], BF16, OFF_HSP)
    sqb = A("sqb", [128, 16384], BF16, OFF_HSP + 32768)
    hsp_fm = A("hsp_fm", [128, 64 * 512], BF16, OFF_HSP)
    wslotU = A("wslotU", [128, 2 * 4096], BF16, OFF_SCRAP)
    eflat = A("eflat", [1, 2048], BF16, OFF_ZB + 4096)
    z = A("z", [128, 16384], BF16, OFF_ZB + 8192)
    xnorm2 = A("xnorm2", [128, 8192], BF16, OFF_ZB + 8192)
    wslotG = A("wslotG", [128, 2 * 4096], BF16, OFF_ZB + 24576)
    wslotD = A("wslotD", [128, 2 * 8192], BF16, OFF_ZB + 8192)
    xnorm = A("xnorm", [128, 8192], BF16, OFF_FB)
    Ds = A("Ds", [128, 2048], F32, OFF_FB)
    ca_bc = A("ca_bc", [128, 2048], F32, OFF_FB + 8192)
    xcg = A("xcg", [128, 4096], BF16, OFF_FB)
    zs = A("zs", [128, 8192], BF16, OFF_FB)
    dsk = A("dsk", [128, 4096], BF16, OFF_FB + 8192)
    ffs = A("ffs", [128, 6144], BF16, OFF_FB)
    yfm = A("yfm", [128, 8192], BF16, OFF_SCRAP)

    kb.sbuf_ptr = OFF_CONST
    cb = {}
    for nm in ("triu", "ident_bf", "ident_f", "ones_bf", "ones_f", "onescol_bf",
               "biasvals", "bias_a", "obias_a", "convw_a", "convb_a", "convw_b",
               "convb_b", "lnw_a", "lnb_a", "normw_b", "dprm", "dtb_a", "alog_a",
               "dtb_b", "alog_b"):
        cb[nm] = kb.alloc(nm + "_s", list(P[nm].shape), P[nm].dtype)
    negA_a = kb.alloc("negA_a", [64, 1], F32)
    negA_b = kb.alloc("negA_b", [64, 1], F32)
    tail_a = kb.alloc("tail_a", [128, 99], BF16)
    tail_b = kb.alloc("tail_b", [128, 99], BF16)
    dtw1_off = kb.sbuf_ptr
    dtw1 = kb.alloc("dtw1", [64, 512], F32)
    ytm = A("ytm", [128, 1024], BF16, dtw1_off)
    lndt = kb.alloc("lndt", [64, 512], F32)
    afm = kb.alloc("afm", [64, 512], F32)
    cafm_off = kb.sbuf_ptr
    cafm = kb.alloc("cafm", [64, 256], F32)
    cwork_off = kb.sbuf_ptr
    cwork = kb.alloc("cwork", [64, 256], F32)
    cwork2 = kb.alloc("cwork2", [64, 256], F32)
    mubc = A("mubc", [128, 512], BF16, cafm_off)
    rsbc = A("rsbc", [128, 512], BF16, cwork_off)
    a_tm = A("a_tm", [128, 64], F32, OFF_ZB + 2368)
    cs_tm = A("cs_tm", [128, 128], F32, OFF_ZB + 2624)
    coeff_tm = A("coeff_tm", [128, 128], F32, OFF_ZB + 3136)
    B_tm = kb.alloc("B_tm", [128, 128], BF16)
    C_sc = kb.alloc("C_sc", [128, 512], BF16)
    E_fm = kb.alloc("E_fm", [64, 256], BF16)

    CBTm = kb.alloc("CBTm", [128, 512], BF16)
    C0 = kb.alloc("C0", [128, 256], BF16)
    stats_off = kb.sbuf_ptr
    stats = kb.alloc("stats", [1, 1024], F32)
    dtbuf = A("dtbuf", [128, 512], F32, stats_off)
    esrow = A("esrow", [1, 64], F32, stats_off + 2048)
    esbc = A("esbc", [128, 64], F32, stats_off + 2304)
    assert kb.sbuf_ptr <= 229247, kb.sbuf_ptr

    nc.bump_sbuf(kb.sbuf_ptr - 16512)
    PSA = nc.alloc_psum_tensor("PSA", [128, 2048], F32)
    PSB = nc.alloc_psum_tensor("PSB", [128, 2048], F32)
    kb.T = dict(locals())
    return nc, kb



def rview(ap, spec, **kw):
    return ap.rearrange(spec, **kw)


def bc3(ap2d, inner):
    return ap2d.rearrange("p (h o) -> p h o", o=1).broadcast_to(
        [ap2d.shape[0], ap2d.shape[1], inner])


def bc_row(ap2d, mid):
    return ap2d.rearrange("p (o n) -> p o n", o=1).broadcast_to(
        [ap2d.shape[0], mid, ap2d.shape[1]])


class Prog:
    def __init__(self, nc, kb):
        self.nc = nc
        self.kb = kb
        self.T = kb.T
        for s in ("W", "S", "P", "V", "A", "O", "G"):
            kb.sem[s] = nc.alloc_semaphore(name="sem" + s)
        self.slot_free = {}
        self.bank_epoch = {}
        self.bank_read = {}
        self.epoch_ctr = 0
        self.ca_slot_cp = {}
        self.xev_acp = {}
        self.conv_done_cp = 0
        self.mfree_cp = {}
        self.phase_gate_p = 0
        self.ebc_free = {}
        self.ytm_free = 0
        self.s_upd_cp = 0

    def new_epoch(self):
        self.epoch_ctr += 1
        return self.epoch_ctr

    @staticmethod
    def _banks(out):
        off = int(out.offset)
        ncols = int(out.ap[-1][1])
        pbase = off // 2048
        b0 = (off % 2048) // 512
        b1 = ((off % 2048) + ncols - 1) // 512
        return [(pbase, b) for b in range(b0, b1 + 1)]

    def mm(self, tname, out, lhsT, rhs, epoch, stop=True, sem=None, waits=()):
        kb = self.kb
        banks = self._banks(out)
        start = False
        xw = list(waits)
        for (pb_, b) in banks:
            key = (tname, pb_, b)
            if self.bank_epoch.get(key) != epoch:
                start = True
                self.bank_epoch[key] = epoch
                for s, cp in self.bank_read.pop((tname, b), {}).items():
                    xw.append((s, cp))
        assert len(banks) == 1 or not start or True
        kb.pe_labels.append(kb.phase)
        kb.op("tensor",
              lambda e, o=out, l=lhsT, r=rhs, st=start, sp=stop:
              e.matmul(out=o, lhsT=l, rhs=r, start=st, stop=sp,
                       skip_group_check=True),
              sem=sem, waits=xw)

    def note_read(self, tname, out_ap, sem, cp):
        for (pb_, b) in self._banks(out_ap):
            d = self.bank_read.setdefault((tname, b), {})
            d[sem] = max(d.get(sem, 0), cp)

    def evac(self, eng, fn, tname, src_ap, sem, waits=()):
        kb = self.kb
        kb.op(eng, fn, sem=sem, waits=waits)
        self.note_read(tname, src_ap, sem, kb.c[sem])
        return kb.c[sem]

    def reset_slots(self, *prefixes):
        for k in list(self.slot_free):
            if k[0] in prefixes:
                del self.slot_free[k]

    def load_strip(self, wsrc, sidx, slot_ap, slot_key):
        kb = self.kb
        waits = []
        if self.slot_free.get(slot_key):
            waits.append(self.slot_free[slot_key])
        else:
            waits += [("P", kb.c["P"]), ("V", kb.c["V"]), ("A", kb.c["A"])]
        if sidx < 4:
            waits.append(("P", self.phase_gate_p))
        kb.op("gpsimd",
              lambda e, d=slot_ap, s=wsrc[sidx]: e.dma_start(out=d, in_=s),
              sem="W", d=16, waits=waits)
        return kb.c["W"]

    def preamble(self):
        kb, T = self.kb, self.T
        P, cb = T["P"], T["cb"]
        for nm in cb:
            kb.op("gpsimd", lambda e, d=cb[nm][:], s=P[nm]: e.dma_start(out=d, in_=s),
                  sem="W", d=16)
        wcp = kb.c["W"]
        for (src, dst) in ((cb["alog_a"], T["negA_a"]), (cb["alog_b"], T["negA_b"])):
            kb.op("scalar", lambda e, s=src[:], d=dst[:]: e.activation(
                out=d, in_=s, func=AF.Exp), sem="A", waits=[("W", wcp)])
        acp = kb.c["A"]
        for dst in (T["negA_a"], T["negA_b"]):
            kb.op("vector", lambda e, d=dst[:]: e.tensor_scalar(
                out=d, in0=d, scalar1=-1.0, scalar2=None, op0=ALU.mult),
                sem="V", waits=[("A", acp)])
        for dst in (T["tail_a"], T["tail_b"]):
            kb.op("vector", lambda e, d=dst[:]: e.memset(d, 0.0), sem="V")
        kb.op("vector", lambda e: e.memset(T["S_ab"][:], 0.0), sem="V")
        kb.op("vector", lambda e: e.memset(T["S_bb"][:], 0.0), sem="V")

    def ln_x(self, dst):
        kb, T = self.kb, self.T
        kb.phase = "ln"
        x_sb, sqb, stats = T["x_sb"], T["sqb"], T["stats"]
        PSB = T["PSB"]
        onescol = T["cb"]["onescol_bf"]
        kb.barrier()
        self.phase_gate_p = kb.c["P"]
        vcp0 = kb.c["V"]
        kb.op("sync", lambda e, d=dst[:], s=x_sb[:]: e.dma_start(out=d, in_=s),
              sem="S", d=16, waits=[("V", vcp0), ("P", kb.c["P"])])
        ccp = kb.c["S"]
        ep = self.new_epoch()
        for dc in range(16):
            self.mm("PSB", PSB[0:1, 0:512], onescol[:], dst[:, dc * 512:(dc + 1) * 512],
                    ep, stop=(dc == 15), sem="P" if dc == 15 else None,
                    waits=[("S", ccp)] if dc == 0 else ())
        kb.op("scalar", lambda e, d=sqb[:, 0:8192], s=dst[:]: e.activation(
            out=d, in_=s, func=AF.Square), sem="A", waits=[("S", ccp)])
        sqcp = kb.c["A"]
        ep2 = self.new_epoch()
        for dc in range(16):
            self.mm("PSB", PSB[0:1, 512:1024], onescol[:],
                    sqb[:, dc * 512:(dc + 1) * 512],
                    ep2, stop=(dc == 15), sem="P" if dc == 15 else None,
                    waits=[("A", sqcp)] if dc == 0 else ())
        pcp = kb.c["P"]
        self._stats(pcp, sub_mean=True, n=2048.0)
        self._normalize(dst, 16, 512, sub_mean=True)

    def _stats(self, pcp, sub_mean, n):
        kb, T = self.kb, self.T
        stats, PSB = T["stats"], T["PSB"]
        self.evac("vector", lambda e: e.tensor_scalar(
            out=stats[0:1, 0:512], in0=PSB[0:1, 0:512], scalar1=1.0 / n,
            scalar2=None, op0=ALU.mult), "PSB", PSB[0:1, 0:512], "V",
            waits=[("P", pcp)])
        if sub_mean:
            kb.op("vector", lambda e: e.tensor_tensor(
                out=stats[0:1, 512:1024], in0=stats[0:1, 0:512],
                in1=stats[0:1, 0:512], op=ALU.mult), sem="V")
            self.evac("vector", lambda e: e.scalar_tensor_tensor(
                out=stats[0:1, 512:1024], in0=PSB[0:1, 512:1024], scalar=1.0 / n,
                in1=stats[0:1, 512:1024], op0=ALU.mult, op1=ALU.subtract),
                "PSB", PSB[0:1, 512:1024], "V", waits=[("P", pcp)])
        else:
            self.evac("vector", lambda e: e.tensor_scalar(
                out=stats[0:1, 512:1024], in0=PSB[0:1, 512:1024], scalar1=1.0 / n,
                scalar2=None, op0=ALU.mult), "PSB", PSB[0:1, 512:1024], "V",
                waits=[("P", pcp)])
        vcp = kb.c["V"]
        bv = T["cb"]["biasvals"]
        kb.op("scalar", lambda e: e.activation(
            out=stats[0:1, 512:1024], in_=stats[0:1, 512:1024], func=AF.Ln,
            bias=bv[0:1, 0:1]), sem="A", waits=[("V", vcp)])
        kb.op("scalar", lambda e: e.activation(
            out=stats[0:1, 512:1024], in_=stats[0:1, 512:1024], func=AF.Exp,
            scale=-0.5), sem="A")

    def _normalize(self, dst, nblk, blk, sub_mean, zview=None):
        kb, T = self.kb, self.T
        stats, PSA, mubc, rsbc = T["stats"], T["PSA"], T["mubc"], T["rsbc"]
        onesf = T["cb"]["ones_f"]
        acp0 = kb.c["A"]
        ep = self.new_epoch()
        if sub_mean:
            self.mm("PSA", PSA[:, 0:512], onesf[:],
                    stats[0:1, 0:512], ep, waits=[("A", acp0)])
        self.mm("PSA", PSA[:, 512:1024], onesf[:],
                stats[0:1, 512:1024], ep, sem="P", waits=[("A", acp0)])
        pcp = kb.c["P"]
        if sub_mean:
            self.evac("scalar", lambda e: e.copy(out=mubc[:], in_=PSA[:, 0:512]),
                      "PSA", PSA[:, 0:512], "A", waits=[("P", pcp)])
        self.evac("scalar", lambda e: e.copy(out=rsbc[:], in_=PSA[:, 512:1024]),
                  "PSA", PSA[:, 512:1024], "A", waits=[("P", pcp)])
        acp = kb.c["A"]
        if zview is None:
            dv = rview(dst[:, 0:nblk * blk], "p (kc t) -> p kc t", t=blk)
            mv = bc_row(mubc[:], nblk)
            rv = bc_row(rsbc[:], nblk)
        else:
            dv, mv, rv = zview
        if sub_mean:
            kb.op("vector", lambda e, dv=dv, mv=mv: e.tensor_tensor(
                out=dv, in0=dv, in1=mv, op=ALU.subtract), sem="V",
                waits=[("A", acp)])
        kb.op("vector", lambda e, dv=dv, rv=rv: e.tensor_tensor(
            out=dv, in0=dv, in1=rv, op=ALU.mult), sem="V", waits=[("A", acp)])

    def in_proj(self, mix):
        kb, T = self.kb, self.T
        kb.phase = "inproj"
        a = mix == "a"
        wsrc = T["W"]["wA" if a else "wB"]
        slot = T["wslotA"]
        xn, z, raw, dtbuf, PSA = T["xnorm"], T["z"], T["raw"], T["dtbuf"], T["PSA"]
        bias = T["cb"]["bias_a"] if a else None
        NCH = 2 if a else 4
        kb.barrier()
        self.phase_gate_p = kb.c["P"]
        self.reset_slots("A")
        zv = rview(z[:], "p (ch dcz l) -> p ch dcz l", ch=NCH, dcz=32)
        xn_cp = kb.c["V"]
        for s in range(66):
            wcp = self.load_strip(wsrc, s, slot[:, (s % 4) * 2048:(s % 4 + 1) * 2048],
                                  ("A", s % 4))
            ep = self.new_epoch()
            for kc in range(16):
                w0 = [("W", wcp), ("V", xn_cp)] if kc == 0 else ()
                self.mm("PSA", PSA[:, (s % 4) * 512:(s % 4 + 1) * 512],
                        slot[:, (s % 4) * 2048 + kc * 128:(s % 4) * 2048 + kc * 128 + 128],
                        xn[:, kc * 512:(kc + 1) * 512], ep,
                        stop=(kc == 15), sem="P" if kc == 15 else None, waits=w0)
            self.slot_free[("A", s % 4)] = ("P", kb.c["P"])
            pcp = kb.c["P"]
            pb = PSA[:, (s % 4) * 512:(s % 4 + 1) * 512]
            if s < 32:
                ov = zv[:, :, s, :]
                iv = rview(pb, "p (ch l) -> p ch l", ch=NCH)
            elif s < 65:
                dcx = s - 32
                ov = raw[:, dcx * 516 + 3:dcx * 516 + 515]
                iv = pb
            else:
                ov = dtbuf[:]
                iv = pb
            if a:
                self.evac("scalar", lambda e, o=ov, i=iv, b=bias[:, s:s + 1]:
                          e.add(out=o, in_=i, add=b), "PSA", pb, "A",
                          waits=[("P", pcp)])
            else:
                self.evac("scalar", lambda e, o=ov, i=iv: e.copy(out=o, in_=i),
                          "PSA", pb, "A", waits=[("P", pcp)])

    def dt_prep(self, mix):
        kb, T = self.kb, self.T
        a = mix == "a"
        dtbuf, dtw1, lndt, afm = T["dtbuf"], T["dtw1"], T["lndt"], T["afm"]
        dtb = T["cb"]["dtb_a" if a else "dtb_b"]
        negA = T["negA_a" if a else "negA_b"]
        acp = kb.c["A"]
        kb.op("vector", lambda e: e.tensor_scalar(
            out=dtw1[:], in0=dtbuf[0:64, :], scalar1=dtb[:, 0:1], scalar2=None,
            op0=ALU.add), sem="V", waits=[("A", acp)])
        kb.op("vector", lambda e: e.tensor_scalar(
            out=afm[:], in0=dtw1[:], scalar1=-1.0, scalar2=None,
            op0=ALU.mult), sem="V")
        kb.op("vector", lambda e: e.tensor_tensor(
            out=lndt[:], in0=dtw1[:], in1=afm[:], op=ALU.max), sem="V")
        vcp = kb.c["V"]
        kb.op("scalar", lambda e: e.activation(
            out=lndt[:], in_=lndt[:], func=AF.Exp, scale=-1.0), sem="A",
            waits=[("V", vcp)])
        bv = T["cb"]["biasvals"]
        kb.op("scalar", lambda e: e.activation(
            out=lndt[:], in_=lndt[:], func=AF.Ln, bias=bv[0:64, 1:2]), sem="A")
        acp2 = kb.c["A"]
        kb.op("vector", lambda e: e.tensor_scalar(
            out=afm[:], in0=dtw1[:], scalar1=0.0, scalar2=None, op0=ALU.max),
            sem="V")
        kb.op("vector", lambda e: e.tensor_tensor(
            out=dtw1[:], in0=afm[:], in1=lndt[:], op=ALU.add), sem="V",
            waits=[("A", acp2)])
        vcp2 = kb.c["V"]
        kb.op("scalar", lambda e: e.activation(
            out=lndt[:], in_=dtw1[:], func=AF.Ln), sem="A", waits=[("V", vcp2)])
        acp3 = kb.c["A"]
        kb.op("vector", lambda e: e.tensor_scalar(
            out=afm[:], in0=dtw1[:], scalar1=negA[:, 0:1], scalar2=None,
            op0=ALU.mult), sem="V", waits=[("A", acp3)])

    def conv_phase(self, mix):
        kb, T = self.kb, self.T
        kb.phase = "conv"
        a = mix == "a"
        raw, conv, tail = T["raw"], T["conv"], T["tail_a" if a else "tail_b"]
        cw = T["cb"]["convw_a" if a else "convw_b"]
        cbias = T["cb"]["convb_a" if a else "convb_b"]
        kb.barrier()
        self.phase_gate_p = kb.c["P"]
        tv = rview(tail[:], "p (dc k) -> p dc k", k=3)
        rv0 = rview(raw[:], "p (dc c) -> p dc c", c=516)
        cvv = rview(conv[:], "p (dc c) -> p dc c", c=512)
        kb.op("vector", lambda e: e.tensor_copy(out=rv0[:, :, 0:3], in_=tv),
              sem="V")
        vtail = kb.c["V"]
        kb.wait("gpsimd", "V", vtail)
        for k in range(4):
            for dc in range(33):
                scl = cw[:, dc * 4 + k:dc * 4 + k + 1]
                ivd = rv0[:, dc, k:k + 512]
                ovd = cvv[:, dc, :]
                eng, sm = ("vector", "V")
                if k == 0:
                    kb.op(eng, lambda e, o=ovd, i=ivd, s=scl: e.tensor_scalar(
                        out=o, in0=i, scalar1=s, scalar2=None, op0=ALU.mult),
                        sem=sm if dc >= 31 else None)
                else:
                    kb.op(eng, lambda e, o=ovd, i=ivd, s=scl: e.scalar_tensor_tensor(
                        out=o, in0=i, scalar=s, in1=o, op0=ALU.mult, op1=ALU.add),
                        sem=sm if dc >= 31 else None)
        kb.op("vector", lambda e: e.tensor_copy(out=tv, in_=rv0[:, :, 512:515]),
              sem="V")
        vcp2 = kb.c["V"]
        gcp2 = kb.c["G"]
        for dc in range(33):
            ov = cvv[:, dc, :]
            kb.op("scalar", lambda e, o=ov, b=cbias[:, dc:dc + 1]: e.activation(
                out=o, in_=o, func=AF.Silu, bias=b),
                sem="A" if dc == 32 else None,
                waits=[("V", vcp2), ("G", gcp2)] if dc == 0 else ())
        self.conv_done_cp = kb.c["A"]

    def ssd_chunk(self, mix, ch):
        kb, T = self.kb, self.T
        kb.phase = "ssdprep_" + mix
        a = mix == "a"
        Q = QA if a else QB
        nh = NHA if a else NHB
        hp = PA if a else PB
        LT = Q // 128
        NG = 4
        GH = nh // NG
        conv, X_tm, Mbuf, yfm, ytm = T["conv"], T["X_tm"], T["Mbuf"], T["yfm"], T["ytm"]
        afm, lndt, cafm, cwork, cwork2 = T["afm"], T["lndt"], T["cafm"], T["cwork"], T["cwork2"]
        a_tm, cs_tm, coeff_tm = T["a_tm"], T["cs_tm"], T["coeff_tm"]
        B_tm, C_sc, E_fm, esrow, esbc = T["B_tm"], T["C_sc"], T["E_fm"], T["esrow"], T["esbc"]
        C0, CBTm, eflat, Ds, ca_bc = T["C0"], T["CBTm"], T["eflat"], T["Ds"], T["ca_bc"]
        xcg, S_ab, dscr, zbuf, zs, dsk = T["xcg"], T["S_ab"], T["dscr"], T["z"], T["zs"], T["dsk"]
        S_st = S_ab if a else T["S_bb"]
        dprm = T["cb"]["dprm"]
        PSA, PSB = T["PSA"], T["PSB"]
        idf, idb = T["cb"]["ident_f"], T["cb"]["ident_bf"]
        onesf, onesb = T["cb"]["ones_f"], T["cb"]["ones_bf"]
        triu = T["cb"]["triu"]
        c0 = ch * Q
        entry_v, entry_a = kb.c["V"], kb.c["A"]
        aConv = self.conv_done_cp

        vA0 = kb.c["V"]
        pA = []
        for lt in range(LT):
            self.mm("PSB", PSB[:, lt * 64:lt * 64 + nh],
                    afm[0:nh, c0 + lt * 128:c0 + lt * 128 + 128],
                    idf[0:nh, 0:nh], self.new_epoch(), sem="P", waits=[("V", vA0)])
        pcp = kb.c["P"]
        for lt in range(LT):
            dst = a_tm[:, lt * 32:lt * 32 + nh] if a else a_tm[:, 0:64]
            self.evac("vector", lambda e, d=dst, s=PSB[:, lt * 64:lt * 64 + nh]:
                      e.tensor_copy(out=d, in_=s), "PSB", PSB[:, lt * 64:lt * 64 + nh],
                      "V", waits=[("P", pcp)])
        vcp = kb.c["V"]
        epc = self.new_epoch()
        for lt in range(LT):
            lhs = a_tm[:, lt * 32:lt * 32 + nh] if a else a_tm[:, 0:64]
            self.mm("PSB", PSB[0:nh, 1024:1024 + Q], lhs,
                    triu[:, lt * 256:lt * 256 + Q], epc, stop=(lt == LT - 1),
                    sem="P" if lt == LT - 1 else None,
                    waits=[("V", vcp)] if lt == 0 else ())
        pcp = kb.c["P"]
        self.evac("vector", lambda e: e.tensor_copy(
            out=cafm[0:nh, 0:Q], in_=PSB[0:nh, 1024:1024 + Q]),
            "PSB", PSB[0:nh, 1024:1024 + Q], "V", waits=[("P", pcp)])
        vca = kb.c["V"]

        if SSD_CUT == 11:
            return
        kb.op("scalar", lambda e: e.activation(out=E_fm[0:nh, 0:Q],
              in_=cafm[0:nh, 0:Q], func=AF.Exp), sem="A", waits=[("V", vca)])
        aE = kb.c["A"]
        dsl = dscr[ch % 2]
        w = [("V", vca)]
        if self.ca_slot_cp.get(ch % 2):
            w.append(("S", self.ca_slot_cp[ch % 2]))
        kb.op("sync", lambda e: e.dma_start(out=dsl[0:nh, 0:Q], in_=cafm[0:nh, 0:Q]),
              sem="S", d=16, waits=w)
        sca = kb.c["S"]
        kb.op("vector", lambda e: e.tensor_tensor(
            out=cwork[0:nh, 0:Q], in0=cafm[0:nh, 0:Q],
            in1=lndt[0:nh, c0:c0 + Q], op=ALU.subtract), sem="V")
        vcs = kb.c["V"]
        for lt in range(LT):
            self.mm("PSB", PSB[:, 512 + lt * 64:512 + lt * 64 + nh],
                    cwork[0:nh, lt * 128:lt * 128 + 128], idf[0:nh, 0:nh],
                    self.new_epoch(), sem="P", waits=[("V", vcs)])
        pcp = kb.c["P"]
        for lt in range(LT):
            src = PSB[:, 512 + lt * 64:512 + lt * 64 + nh]
            self.evac("vector", lambda e, lt=lt, s=src: e.tensor_copy(
                out=cs_tm[:, lt * 64:lt * 64 + nh], in_=s), "PSB", src, "V",
                waits=[("P", pcp)])
        kb.op("vector", lambda e: e.tensor_scalar(
            out=cwork2[0:nh, 0:Q], in0=cafm[0:nh, 0:Q],
            scalar1=cafm[0:nh, Q - 1:Q], scalar2=None, op0=ALU.subtract), sem="V")
        kb.op("vector", lambda e: e.tensor_tensor(
            out=cwork2[0:nh, 0:Q], in0=lndt[0:nh, c0:c0 + Q],
            in1=cwork2[0:nh, 0:Q], op=ALU.subtract), sem="V")
        vco = kb.c["V"]
        kb.op("scalar", lambda e: e.activation(out=cwork2[0:nh, 0:Q],
              in_=cwork2[0:nh, 0:Q], func=AF.Exp), sem="A", waits=[("V", vco)])
        aco = kb.c["A"]
        for lt in range(LT):
            self.mm("PSB", PSB[:, 640 + lt * 64:640 + lt * 64 + nh],
                    cwork2[0:nh, lt * 128:lt * 128 + 128], idf[0:nh, 0:nh],
                    self.new_epoch(), sem="P", waits=[("A", aco)])
        pcp = kb.c["P"]
        for lt in range(LT):
            src = PSB[:, 640 + lt * 64:640 + lt * 64 + nh]
            self.evac("vector", lambda e, lt=lt, s=src: e.tensor_copy(
                out=coeff_tm[:, lt * 64:lt * 64 + nh], in_=s), "PSB", src, "V",
                waits=[("P", pcp)])
        self.mm("PSB", PSB[0:1, 896:896 + nh], cafm[0:nh, Q - 1:Q],
                idf[0:nh, 0:nh], self.new_epoch(), sem="P", waits=[("V", vca)])
        pcp = kb.c["P"]
        self.evac("vector", lambda e: e.tensor_copy(out=esrow[0:1, 0:nh],
                  in_=PSB[0:1, 896:896 + nh]), "PSB", PSB[0:1, 896:896 + nh],
                  "V", waits=[("P", pcp)])
        ves = kb.c["V"]
        kb.op("scalar", lambda e: e.activation(out=esrow[0:1, 0:nh],
              in_=esrow[0:1, 0:nh], func=AF.Exp), sem="A", waits=[("V", ves)])
        aes = kb.c["A"]
        self.mm("PSB", PSB[:, 896:896 + nh], onesf[0:1, 0:128],
                esrow[0:1, 0:nh], self.new_epoch(), sem="P", waits=[("A", aes)])
        pcp = kb.c["P"]
        self.evac("vector", lambda e: e.tensor_copy(out=esbc[0:64, 0:nh],
                  in_=PSB[0:64, 896:896 + nh]), "PSB", PSB[:, 896:896 + nh],
                  "V", waits=[("P", pcp)])
        v_esbc = kb.c["V"]

        if SSD_CUT == 12:
            return
        self.mm("PSB", PSB[0:64, 0:Q], idb[64:128, 64:128],
                conv[64:128, 32 * 512 + c0:32 * 512 + c0 + Q],
                self.new_epoch(), sem="P", waits=[("A", aConv)])
        pcp = kb.c["P"]
        self.evac("vector", lambda e: e.tensor_copy(out=C0[0:64, 0:Q],
                  in_=PSB[0:64, 0:Q]), "PSB", PSB[0:64, 0:Q], "V",
                  waits=[("P", pcp)])
        vC0 = kb.c["V"]
        if SSD_CUT == 15:
            return
        for st in range(LT):
            lhsB = conv[0:64, 32 * 512 + c0 + st * 128:32 * 512 + c0 + st * 128 + 128]
            rhsC = C0[0:64, 0:Q]
            self.mm("PSB", PSB[:, 512 + st * 256:512 + st * 256 + Q],
                    lhsB, rhsC, self.new_epoch(), sem="P",
                    waits=[("V", vC0)])
        pcp = kb.c["P"]
        for st in range(LT):
            src = PSB[:, 512 + st * 256:512 + st * 256 + Q]
            ov = rview(CBTm[:, st * 256:st * 256 + Q], "p (o l) -> p o l", o=1)
            iv = rview(src, "p (o l) -> p o l", o=1)
            tvv = rview(triu[:, st * 256:st * 256 + Q], "p (o l) -> p o l", o=1)
            self.evac("vector", lambda e, ov=ov, iv=iv, tvv=tvv: e.tensor_tensor(
                out=ov, in0=iv, in1=tvv, op=ALU.mult), "PSB", src, "V",
                waits=[("P", pcp)])
        vCBT = kb.c["V"]
        if SSD_CUT == 16:
            return
        for lt in range(LT):
            self.mm("PSB", PSB[:, lt * 64:lt * 64 + 64],
                    conv[0:64, 32 * 512 + c0 + lt * 128:32 * 512 + c0 + lt * 128 + 128],
                    idb[0:64, 0:64], self.new_epoch(), sem="P",
                    waits=[("A", aConv)])
        pcp = kb.c["P"]
        for lt in range(LT):
            src = PSB[:, lt * 64:lt * 64 + 64]
            self.evac("vector", lambda e, lt=lt, s=src: e.tensor_copy(
                out=B_tm[:, lt * 64:lt * 64 + 64], in_=s), "PSB", src, "V",
                waits=[("P", pcp)])
        vBtm = kb.c["V"]
        if SSD_CUT == 13:
            return
        for tt in range(LT):
            for dg in range(8):
                xoff = 1024 + 512 * ((tt * 8 + dg) % 2)
                epx = self.new_epoch()
                for j in range(4):
                    dc = dg * 4 + j
                    self.mm("PSB", PSB[:, xoff + j * 128:xoff + j * 128 + 128],
                            conv[:, dc * 512 + c0 + tt * 128:dc * 512 + c0 + tt * 128 + 128],
                            idb[:], epx, sem="P" if j == 3 else None,
                            waits=[("A", aConv)] if j == 0 else ())
                pcp = kb.c["P"]
                self.evac("scalar", lambda e, tt=tt, dg=dg, xoff=xoff: e.copy(
                    out=X_tm[:, tt * 4096 + dg * 512:tt * 4096 + dg * 512 + 512],
                    in_=PSB[:, xoff:xoff + 512]), "PSB", PSB[:, xoff:xoff + 512], "A",
                    waits=[("P", pcp)])
        aXtm = kb.c["A"]

        if SSD_CUT == 1:
            return
        kb.phase = "ssdY_" + mix
        zofs = ch * 32 * Q
        y_ep = {}
        tt2cp = {}
        g_pcp = {}
        for g in range(NG):
            w = [("A", aE)]
            if g >= 1:
                w.append(("P", g_pcp.get(g - 1, 0)))
            kb.op("sync", lambda e, g=g: e.dma_start(
                out=eflat[0:1, :], in_=E_fm[g * GH:(g + 1) * GH, 0:Q]),
                sem="S", d=16, waits=w)
            s_ef = kb.c["S"]
            seg = dsl[g * GH:(g + 1) * GH, 0:Q]
            seg = seg.rearrange("(o h) l -> o h l", o=1).broadcast_to([128, GH, Q])
            dstv = rview(ca_bc[:, 0:GH * Q], "p (h l) -> p h l", h=GH)
            w = [("S", sca), ("V", entry_v), ("A", entry_a)]
            if g >= 1:
                w.append(("V", tt2cp.get(g - 1, 0)))
            kb.op("sync", lambda e, seg=seg, dstv=dstv: e.dma_start(out=dstv, in_=seg),
                  sem="S", d=16, waits=w)
            s_rep = kb.c["S"]
            self.ca_slot_cp[ch % 2] = kb.c["S"]
            buf = g % 2
            mslc = []
            for st in range(LT):
                dv = rview(Ds[:], "p (h l) -> p h l", h=GH)
                cav = rview(ca_bc[:, 0:GH * Q], "p (h l) -> p h l", h=GH)
                csv = bc3(cs_tm[:, st * 64 + g * GH:st * 64 + (g + 1) * GH], Q)
                w = [("S", s_rep), ("V", vcs + 0), ("A", entry_a)]
                if (buf, st) in self.mfree_cp:
                    w.append(("P", self.mfree_cp[(buf, st)]))
                kb.op("vector", lambda e, dv=dv, cav=cav, csv=csv: e.tensor_tensor(
                    out=dv, in0=cav, in1=csv, op=ALU.subtract), sem="V", waits=w)
                kb.op("vector", lambda e: e.tensor_scalar(
                    out=Ds[:], in0=Ds[:], scalar1=80.0, scalar2=None,
                    op0=ALU.min), sem="V")
                vD = kb.c["V"]
                mofs = buf * 4096 + st * 2048
                msl = Mbuf[:, mofs:mofs + GH * Q]
                kb.op("scalar", lambda e, o=msl: e.activation(
                    out=o, in_=Ds[:, 0:GH * Q], func=AF.Exp), sem="A",
                    waits=[("V", vD)])
                aM = kb.c["A"]
                mv = rview(msl, "p (h l) -> p h l", h=GH)
                cbv = bc_row(CBTm[:, st * 256:st * 256 + Q], GH)
                kb.op("vector", lambda e, mv=mv, cbv=cbv: e.tensor_tensor(
                    out=mv, in0=mv, in1=cbv, op=ALU.mult), sem="V",
                    waits=[("A", aM), ("V", vCBT)])
                mslc.append(msl)
            tt2cp[g] = kb.c["V"]
            if SSD_CUT == 2:
                g_pcp[g] = kb.c["P"]
                for st in range(LT):
                    self.mfree_cp[(buf, st)] = kb.c["V"]
                continue
            epY = [self.new_epoch() for _ in range(LT)]
            C_sc2 = T["C_sc2"]
            GH2 = GH // 2
            vCs_r = {}
            for r in range(2):
                epE = self.new_epoch()
                for i2 in range(GH2):
                    i = r * GH2 + i2
                    self.mm("PSB", PSB[:, 1024 + i2 * Q:1024 + (i2 + 1) * Q],
                            onesb[0:1, 0:128], eflat[0:1, i * Q:(i + 1) * Q], epE,
                            sem="P" if i2 == GH2 - 1 else None,
                            waits=[("S", s_ef)] if i2 == 0 else ())
                pcp = kb.c["P"]
                c0v = bc_row(C0[0:64, 0:Q], GH2)
                epsv = rview(PSB[0:64, 1024:1024 + GH2 * Q], "p (h l) -> p h l", h=GH2)
                csv2 = rview(C_sc2[0:64, r * GH2 * Q:(r + 1) * GH2 * Q],
                             "p (h l) -> p h l", h=GH2)
                self.evac("vector", lambda e, c0v=c0v, epsv=epsv, csv2=csv2:
                          e.tensor_tensor(out=csv2, in0=c0v, in1=epsv, op=ALU.mult),
                          "PSB", PSB[0:64, 1024:1024 + GH2 * Q], "V",
                          waits=[("P", pcp), ("V", vC0)])
                vCs_r[r] = kb.c["V"]
            for i in range(GH):
                h = g * GH + i
                vCs = vCs_r[i // GH2]
                for lt in range(LT):
                    yac = "PSA"
                    ybase = lt * 1024
                    for st in range(LT):
                        w = [("V", tt2cp[g]), ("A", aXtm)] if (st == 0) else ()
                        self.mm(yac, T["PSA"][:, ybase + i * hp:ybase + i * hp + hp],
                                Mbuf[:, buf * 4096 + st * 2048 + i * Q + lt * 128:
                                     buf * 4096 + st * 2048 + i * Q + lt * 128 + 128],
                                X_tm[:, st * 4096 + h * hp:st * 4096 + h * hp + hp],
                                epY[lt], stop=False, waits=w)
                    self.mm(yac, T["PSA"][:, ybase + i * hp:ybase + i * hp + hp],
                            C_sc2[0:64, i * Q + lt * 128:i * Q + lt * 128 + 128],
                            S_st[0:64, h * hp:h * hp + hp], epY[lt], stop=True,
                            sem="P" if (i == GH - 1 and lt == LT - 1) else None,
                            waits=[("V", vCs), ("V", self.s_upd_cp)])
            g_pcp[g] = kb.c["P"]
            for st in range(LT):
                self.mfree_cp[(buf, st)] = g_pcp[g]
            for lt in range(LT):
                src = T["PSA"][:, lt * 1024:lt * 1024 + GH * hp]
                w = [("P", g_pcp[g])]
                if self.ytm_free:
                    w.append(("P", self.ytm_free))
                aY = self.evac("scalar", lambda e, s=src: e.copy(
                    out=ytm[:, 0:GH * hp], in_=s), "PSA", src, "A", waits=w)
                ept = self.new_epoch()
                nslot = 1024 // hp
                for i in range(GH):
                    h = g * GH + i
                    if a:
                        dst = PSB[:, (i % 8) * 128:(i % 8) * 128 + 128]
                    else:
                        base = 64 * (h % 2)
                        dst = PSB[base:base + 64, (i // 2 % 8) * 128:(i // 2 % 8) * 128 + 128]
                    if i % 16 == 0:
                        ept = self.new_epoch()
                    self.mm("PSB", dst, ytm[:, i * hp:i * hp + hp], idb[:],
                            ept, sem="P" if (i % 16 == 15 or i == GH - 1) else None,
                            waits=[("A", aY)] if i == 0 else ())
                    if (i % 16 == 15 or i == GH - 1):
                        pcp = kb.c["P"]
                        self.ytm_free = pcp
                        nle = (i % 16) + 1
                        if a:
                            ysl = rview(yfm[:, 0:8192], "p (dc l) -> p dc l", l=Q)[
                                :, g * GH:g * GH + 8, lt * 128:lt * 128 + 128]
                            src2 = rview(PSB[:, 0:1024], "p (i l) -> p i l", i=8)
                        else:
                            ysl = rview(yfm[:, 0:4096], "p (dc l) -> p dc l", l=128)[
                                :, g * 8:g * 8 + 8, :]
                            src2 = rview(PSB[:, 0:1024], "p (i l) -> p i l", i=8)
                        self.evac("scalar", lambda e, o=ysl, s=src2: e.copy(
                            out=o, in_=s), "PSB", PSB[:, 0:1024], "A",
                            waits=[("P", pcp)])
        ayf = kb.c["A"]
        if SSD_CUT in (2, 3):
            return

        kb.phase = "ssdstates_" + mix
        for half in range(2):
            hh = nh // 2
            ep = self.new_epoch()
            for lt in range(LT):
                cof = bc3(coeff_tm[:, lt * 64 + half * hh:lt * 64 + half * hh + hh], hp)
                xs = rview(X_tm[:, lt * 4096 + half * 2048:lt * 4096 + half * 2048 + 2048],
                           "p (h q) -> p h q", h=hh)
                xg = rview(xcg[:, lt * 2048:lt * 2048 + 2048], "p (h q) -> p h q", h=hh)
                w = [("A", aXtm)]
                if half == 0 and lt == 0:
                    w.append(("P", g_pcp[NG - 1]))
                kb.op("vector", lambda e, xg=xg, xs=xs, cof=cof: e.tensor_tensor(
                    out=xg, in0=xs, in1=cof, op=ALU.mult), sem="V", waits=w)
            vxg = kb.c["V"]
            for lt in range(LT):
                for j in range(4):
                    w = [("V", vxg)]
                    self.mm("PSA", PSA[0:64, j * 512:j * 512 + 512],
                            B_tm[:, lt * 64:lt * 64 + 64],
                            xcg[:, lt * 2048 + j * 512:lt * 2048 + j * 512 + 512],
                            ep, stop=(lt == LT - 1),
                            sem="P" if (lt == LT - 1 and j == 3) else None,
                            waits=w if (lt == 0 and j == 0) else [("V", vBtm)] if False else ())
            pst = kb.c["P"]
            sofs = half * 2048
            ssl = S_st[0:64, sofs:sofs + 2048]
            sv = rview(ssl, "p (h q) -> p h q", h=hh)
            ev = bc3(esbc[0:64, half * hh:half * hh + hh], hp)
            kb.op("vector", lambda e, sv=sv, ev=ev: e.tensor_tensor(
                out=sv, in0=sv, in1=ev, op=ALU.mult), sem="V",
                waits=[("P", g_pcp[NG - 1]), ("V", v_esbc)])
            psl = PSA[0:64, 0:2048]
            self.evac("vector", lambda e, ssl=ssl, psl=psl: e.tensor_tensor(
                out=ssl, in0=ssl, in1=psl, op=ALU.add),
                "PSA", psl, "V", waits=[("P", pst)])
        self.s_upd_cp = kb.c["V"]

        if SSD_CUT == 4:
            return
        if not a:
            xv = rview(conv[:], "p (dc c) -> p dc c", c=512)[:, 0:32, c0:c0 + Q]
            dpv = bc3(dprm[:, 0:32], Q)
            kb.op("vector", lambda e, xv=xv, dpv=dpv: e.tensor_tensor(
                out=dsk[:, 0:4096].rearrange("p (dc l) -> p dc l", dc=32),
                in0=xv, in1=dpv, op=ALU.mult), sem="V", waits=[("A", ayf)])
            kb.op("vector", lambda e: e.tensor_tensor(
                out=yfm[:, 0:4096], in0=yfm[:, 0:4096], in1=dsk[:, 0:4096],
                op=ALU.add), sem="V")
        vdsk = kb.c["V"]

        nz = 32 * Q
        zsl = zbuf[:, zofs:zofs + nz]
        kb.op("scalar", lambda e, zsl=zsl: e.activation(
            out=zs[:, 0:nz], in_=zsl, func=AF.Silu), sem="A",
            waits=[("A", ayf), ("V", vdsk)])
        acp = kb.c["A"]
        kb.op("vector", lambda e, zsl=zsl: e.tensor_tensor(
            out=zsl, in0=yfm[:, 0:nz], in1=zs[:, 0:nz], op=ALU.mult),
            sem="V", waits=[("A", acp)])

    def norm_out(self, mix):
        kb, T = self.kb, self.T
        kb.phase = "normout"
        a = mix == "a"
        Q = QA if a else QB
        NCH = 2 if a else 4
        z, sqb, stats, x_sb = T["z"], T["sqb"], T["stats"], T["x_sb"]
        PSA, PSB = T["PSA"], T["PSB"]
        onescol = T["cb"]["onescol_bf"]
        wsrc = T["W"]["wOa" if a else "wOb"]
        slot = T["wslotO"]
        obias = T["cb"]["obias_a"] if a else None
        lnw = T["cb"]["lnw_a" if a else "normw_b"]
        lnb = T["cb"]["lnb_a"] if a else None
        kb.barrier()
        self.phase_gate_p = kb.c["P"]
        self.reset_slots("O")
        vz = kb.c["V"]
        zv = rview(z[:], "p (ch dcz l) -> p ch dcz l", ch=NCH, dcz=32)
        ep = self.new_epoch()
        for ch in range(NCH):
            for dc in range(32):
                last = dc == 31
                self.mm("PSB", PSB[0:1, ch * Q:(ch + 1) * Q], onescol[:],
                        zv[:, ch, dc, :], ep, stop=last,
                        sem="P" if (last and ch == NCH - 1) else None,
                        waits=[("V", vz)] if (dc == 0 and ch == 0) else ())
        kb.op("scalar", lambda e: e.activation(
            out=sqb[:], in_=z[:], func=AF.Square), sem="A", waits=[("V", vz)])
        acp = kb.c["A"]
        sqv = rview(sqb[:], "p (ch dcz l) -> p ch dcz l", ch=NCH, dcz=32)
        ep2 = self.new_epoch()
        for ch in range(NCH):
            for dc in range(32):
                last = dc == 31
                self.mm("PSB", PSB[0:1, 512 + ch * Q:512 + (ch + 1) * Q],
                        onescol[:], sqv[:, ch, dc, :], ep2, stop=last,
                        sem="P" if (last and ch == NCH - 1) else None,
                        waits=[("A", acp)] if (dc == 0 and ch == 0) else ())
        pcp = kb.c["P"]
        self._stats(pcp, sub_mean=a, n=4096.0)
        mv = bc_mid = None
        mub, rsb = T["mubc"], T["rsbc"]
        zn = rview(z[:], "p (ch dcz l) -> p ch dcz l", ch=NCH, dcz=32)
        muv = rview(mub[:, 0:512], "p (ch l) -> p ch l", ch=NCH)
        muv = muv.rearrange("p ch (o l) -> p ch o l", o=1).broadcast_to([128, NCH, 32, Q])
        rsv = rview(rsb[:, 0:512], "p (ch l) -> p ch l", ch=NCH)
        rsv = rsv.rearrange("p ch (o l) -> p ch o l", o=1).broadcast_to([128, NCH, 32, Q])
        self._normalize(None, 0, 0, sub_mean=a, zview=(zn, muv, rsv))
        vn = kb.c["V"]
        for dc in range(32):
            ov = zv[:, :, dc, :]
            if a:
                kb.op("vector", lambda e, o=ov, w1=lnw[:, dc:dc + 1], w2=lnb[:, dc:dc + 1]:
                      e.tensor_scalar(out=o, in0=o, scalar1=w1, scalar2=w2,
                                      op0=ALU.mult, op1=ALU.add),
                      sem="V" if dc == 31 else None)
            else:
                kb.op("vector", lambda e, o=ov, w1=lnw[:, dc:dc + 1]:
                      e.tensor_scalar(out=o, in0=o, scalar1=w1, scalar2=None,
                                      op0=ALU.mult),
                      sem="V" if dc == 31 else None)
        vaff = kb.c["V"]
        kb.phase = "outproj"
        for s in range(16):
            wcp = self.load_strip(wsrc, s, slot[:, (s % 2) * 4096:(s % 2 + 1) * 4096],
                                  ("O", s % 2))
            ep = self.new_epoch()
            first = True
            for kc in range(32):
                for ch in range(NCH):
                    w0 = [("W", wcp), ("V", vaff)] if first else ()
                    first = False
                    self.mm("PSA", PSA[:, (s % 4) * 512 + ch * Q:(s % 4) * 512 + (ch + 1) * Q],
                            slot[:, (s % 2) * 4096 + kc * 128:(s % 2) * 4096 + kc * 128 + 128],
                            zv[:, ch, kc, :], ep,
                            stop=(kc == 31), sem="P" if (kc == 31 and ch == NCH - 1) else None,
                            waits=w0)
            self.slot_free[("O", s % 2)] = ("P", kb.c["P"])
            pcp = kb.c["P"]
            pb = PSA[:, (s % 4) * 512:(s % 4 + 1) * 512]
            xo = x_sb[:, s * 512:(s + 1) * 512]
            if a:
                self.evac("vector", lambda e, o=xo, i=pb, b=obias[:, s:s + 1]:
                          e.scalar_tensor_tensor(out=o, in0=i, scalar=b, in1=o,
                                                 op0=ALU.add, op1=ALU.add),
                          "PSA", pb, "V", waits=[("P", pcp)])
            else:
                self.evac("vector", lambda e, o=xo, i=pb: e.tensor_tensor(
                    out=o, in0=o, in1=i, op=ALU.add), "PSA", pb, "V",
                    waits=[("P", pcp)])

    def ffn(self):
        kb, T = self.kb, self.T
        kb.phase = "ffn"
        xn, hsp, ffs = T["xnorm2"], T["hsp_fm"], T["ffs"]
        slotG, slotU, slotD = T["wslotG"], T["wslotU"], T["wslotD"]
        PSA, PSB = T["PSA"], T["PSB"]
        idb = T["cb"]["ident_bf"]
        x_sb = T["x_sb"]
        kb.barrier()
        self.phase_gate_p = kb.c["P"]
        self.reset_slots("G", "U", "D")
        vxn = kb.c["V"]
        sg = ffs[:, 0:1024]
        hb = ffs[:, 1024:2048]
        ha = ffs[:, 2048:3072]
        t2 = ffs[:, 3072:3328]
        msk = ffs[:, 3328:4352]
        hsts = [ffs[:, 4352:5376], T["hst2"][:, 0:1024]]
        vhs_log = {}
        tr_done = {}

        def emit_tr(s_):
            hst = hsts[s_ % 2]
            for ts_ in range(4):
                ept = self.new_epoch()
                toff = 512 * (ts_ % 2)
                for j in range(2):
                    self.mm("PSA", PSA[:, toff + j * 128:toff + j * 128 + 128],
                            hst[:, ts_ * 256 + j * 128:ts_ * 256 + j * 128 + 128],
                            idb[:], ept, sem="P" if j == 1 else None,
                            waits=[("V", vhs_log[s_])] if (ts_ == 0 and j == 0) else ())
                pcp = kb.c["P"]
                ic0 = 2 * s_
                ov = rview(hsp[:, 0:64 * 512], "p (ic t) -> p ic t", t=512)[
                    :, ic0:ic0 + 2, ts_ * 128:ts_ * 128 + 128]
                self.evac("scalar", lambda e, ov=ov, toff=toff: e.copy(
                    out=ov, in_=rview(PSA[:, toff:toff + 256], "p (ic t) -> p ic t", t=128)),
                    "PSA", PSA[:, toff:toff + 256], "A", waits=[("P", pcp)])
            tr_done[s_] = kb.c["A"]

        for s_ in range(32):
            wg = self.load_strip(T["W"]["wG"], s_, slotG[:, (s_ % 2) * 4096:(s_ % 2 + 1) * 4096],
                                 ("G", s_ % 2))
            wu = self.load_strip(T["W"]["wU"], s_, slotU[:, (s_ % 2) * 4096:(s_ % 2 + 1) * 4096],
                                 ("U", s_ % 2))
            epg, epu = self.new_epoch(), self.new_epoch()
            gcp, ucp = {}, {}
            for ts_ in range(4):
                for kc in range(16):
                    w0 = [("W", wg), ("V", vxn)] if kc == 0 else ()
                    self.mm("PSB", PSB[:, ts_ * 256:(ts_ + 1) * 256],
                            xn[:, kc * 512 + ts_ * 128:kc * 512 + ts_ * 128 + 128],
                            slotG[:, (s_ % 2) * 4096 + kc * 256:(s_ % 2) * 4096 + (kc + 1) * 256],
                            epg, stop=(kc == 15), sem="P" if kc == 15 else None, waits=w0)
                gcp[ts_] = kb.c["P"]
            for ts_ in range(4):
                for kc in range(16):
                    w0 = [("W", wu)] if kc == 0 else ()
                    self.mm("PSB", PSB[:, 1024 + ts_ * 256:1024 + (ts_ + 1) * 256],
                            xn[:, kc * 512 + ts_ * 128:kc * 512 + ts_ * 128 + 128],
                            slotU[:, (s_ % 2) * 4096 + kc * 256:(s_ % 2) * 4096 + (kc + 1) * 256],
                            epu, stop=(kc == 15), sem="P" if kc == 15 else None, waits=w0)
                ucp[ts_] = kb.c["P"]
            self.slot_free[("G", s_ % 2)] = ("P", kb.c["P"])
            self.slot_free[("U", s_ % 2)] = ("P", kb.c["P"])
            if s_ >= 1:
                emit_tr(s_ - 1)
            for pr_ in range(2):
                psl = PSB[:, pr_ * 512:(pr_ + 1) * 512]
                self.evac("scalar", lambda e, o=sg[:, pr_ * 512:(pr_ + 1) * 512], i=psl:
                          e.activation(out=o, in_=i, func=AF.Silu), "PSB", psl,
                          "A", waits=[("P", gcp[2 * pr_ + 1])])
                acp = kb.c["A"]
                pul = PSB[:, 1024 + pr_ * 512:1024 + (pr_ + 1) * 512]
                self.evac("vector", lambda e, o=hb[:, pr_ * 512:(pr_ + 1) * 512],
                          i0=sg[:, pr_ * 512:(pr_ + 1) * 512], i1=pul:
                          e.tensor_tensor(out=o, in0=i0, in1=i1, op=ALU.mult),
                          "PSB", pul, "V", waits=[("P", ucp[2 * pr_ + 1]), ("A", acp)])
            vH = kb.c["V"]
            kb.op("scalar", lambda e: e.activation(
                out=ha[:], in_=hb[:], func=AF.Square), sem="A", waits=[("V", vH)])
            aAb = kb.c["A"]
            hv = rview(ha[:], "p (gq four) -> p gq four", four=4)
            kb.op("vector", lambda e: e.tensor_tensor(
                out=t2[:, 0:256].rearrange("p (g o) -> p g o", o=1),
                in0=hv[:, :, 0:1], in1=hv[:, :, 1:2], op=ALU.max), sem="V",
                waits=[("A", aAb)])
            kb.op("vector", lambda e: e.tensor_tensor(
                out=msk[:, 0:256].rearrange("p (g o) -> p g o", o=1),
                in0=hv[:, :, 0:1], in1=hv[:, :, 1:2], op=ALU.min), sem="V")
            kb.op("vector", lambda e: e.tensor_tensor(
                out=msk[:, 256:512].rearrange("p (g o) -> p g o", o=1),
                in0=hv[:, :, 2:3], in1=hv[:, :, 3:4], op=ALU.max), sem="V")
            kb.op("vector", lambda e: e.tensor_tensor(
                out=msk[:, 512:768].rearrange("p (g o) -> p g o", o=1),
                in0=hv[:, :, 2:3], in1=hv[:, :, 3:4], op=ALU.min), sem="V")
            kb.op("vector", lambda e: e.tensor_tensor(
                out=t2[:, 0:256], in0=t2[:, 0:256], in1=msk[:, 256:512],
                op=ALU.min), sem="V")
            kb.op("vector", lambda e: e.tensor_tensor(
                out=msk[:, 0:256], in0=msk[:, 0:256], in1=msk[:, 512:768],
                op=ALU.max), sem="V")
            kb.op("vector", lambda e: e.tensor_tensor(
                out=t2[:, 0:256], in0=t2[:, 0:256], in1=msk[:, 0:256],
                op=ALU.max), sem="V")
            t2r = rview(t2[:, 0:256], "p (g o) -> p g o", o=1).broadcast_to([128, 256, 4])
            kb.op("vector", lambda e, t2r=t2r: e.tensor_tensor(
                out=rview(msk[:], "p (g four) -> p g four", four=4),
                in0=hv, in1=t2r, op=ALU.is_ge), sem="V")
            hst = hsts[s_ % 2]
            w_hst = [("A", tr_done[s_ - 2])] if s_ >= 2 else []
            kb.op("vector", lambda e, hst=hst: e.tensor_tensor(
                out=hst[:], in0=hb[:], in1=msk[:], op=ALU.mult), sem="V",
                waits=w_hst)
            vhs_log[s_] = kb.c["V"]
        emit_tr(31)
        ahsp = kb.c["A"]
        kb.phase = "ffndown"
        self.phase_gate_p = kb.c["P"]
        self.reset_slots("D")
        for dt_ in range(16):
            wd = self.load_strip(T["W"]["wD"], dt_, slotD[:, (dt_ % 2) * 8192:(dt_ % 2 + 1) * 8192],
                                 ("D", dt_ % 2))
            ep = self.new_epoch()
            for ic in range(64):
                w0 = [("W", wd), ("A", ahsp)] if ic == 0 else ()
                self.mm("PSA", PSA[:, (dt_ % 2) * 512 + 1024:(dt_ % 2) * 512 + 1536],
                        slotD[:, (dt_ % 2) * 8192 + ic * 128:(dt_ % 2) * 8192 + ic * 128 + 128],
                        hsp[:, ic * 512:(ic + 1) * 512], ep,
                        stop=(ic == 63), sem="P" if ic == 63 else None, waits=w0)
            self.slot_free[("D", dt_ % 2)] = ("P", kb.c["P"])
            pcp = kb.c["P"]
            pb = PSA[:, (dt_ % 2) * 512 + 1024:(dt_ % 2) * 512 + 1536]
            xo = T["x_sb"][:, dt_ * 512:(dt_ + 1) * 512]
            self.evac("vector", lambda e, o=xo, i=pb: e.tensor_tensor(
                out=o, in0=o, in1=i, op=ALU.add), "PSA", pb, "V",
                waits=[("P", pcp)])

    def load_x(self, tile):
        kb, T = self.kb, self.T
        u, x_sb = T["u"], T["x_sb"]
        kb.barrier(engs=("gpsimd",))
        src = rview(u, "(dc p) t -> p dc t", p=128)[:, :, tile * 512:(tile + 1) * 512]
        kb.op("gpsimd", lambda e: e.dma_start(
            out=rview(x_sb[:], "p (dc t) -> p dc t", t=512), in_=src),
            sem="W", d=16, waits=[("O", kb.c["O"])])
        scp = kb.c["W"]
        for e in ("tensor", "vector", "scalar", "sync"):
            kb.wait(e, "W", scp)

    def store_x(self, tile):
        kb, T = self.kb, self.T
        out, x_sb = T["out"], T["x_sb"]
        kb.barrier(engs=("gpsimd",))
        dst = rview(out, "(dc p) t -> p dc t", p=128)[:, :, tile * 512:(tile + 1) * 512]
        kb.op("gpsimd", lambda e: e.dma_start(
            out=dst, in_=rview(x_sb[:], "p (dc t) -> p dc t", t=512)),
            sem="O", d=16)

    def finish(self):
        kb = self.kb
        kb.wait("sync", "O", kb.c["O"])
        kb.wait("gpsimd", "O", kb.c["O"])



def build_program(upto="full"):
    nc, kb = build_core(upto)
    pr = Prog(nc, kb)
    T = kb.T
    pr.preamble()
    stages = {"ln1": 0.5, "res1": 1, "res2": 2, "res3": 3, "full": 4,
              "only_b": 10, "only_ffn": 11}
    lim = stages[upto]
    if lim == 10:
        for tile in range(NTILE):
            pr.load_x(tile)
            pr.ln_x(T["xnorm"])
            pr.in_proj("b")
            if SSD_CUT != 5:
                pr.dt_prep("b")
                if SSD_CUT != 6:
                    pr.conv_phase("b")
                    if SSD_CUT != 7:
                        for ch in range(4):
                            pr.ssd_chunk("b", ch)
                        if SSD_CUT == 0:
                            pr.norm_out("b")
            kb.barrier(engs=("sync",))
            pr.store_x(tile)
        pr.finish()
        kb.check_waits()
        with nc.Block() as block:
            kb.finalize_into_block(block)
        return nc
    if lim == 11:
        for tile in range(NTILE):
            pr.load_x(tile)
            pr.ln_x(T["xnorm2"])
            pr.ffn()
            kb.barrier(engs=("sync",))
            pr.store_x(tile)
        pr.finish()
        kb.check_waits()
        with nc.Block() as block:
            kb.finalize_into_block(block)
        return nc
    for tile in range(NTILE):
        pr.load_x(tile)
        pr.ln_x(T["xnorm"])
        if lim == 0.5:
            vcp = kb.c["V"]
            kb.op("vector", lambda e: e.tensor_copy(out=T["x_sb"][:],
                  in_=T["xnorm"][:]), sem="V", waits=[("V", vcp), ("O", kb.c["O"])])
            kb.barrier(engs=("sync",))
            pr.store_x(tile)
            continue
        pr.in_proj("a")
        pr.dt_prep("a")
        pr.conv_phase("a")
        for ch in range(2):
            pr.ssd_chunk("a", ch)
        pr.norm_out("a")
        if lim == 1:
            kb.barrier(engs=("sync",))
            pr.store_x(tile)
            continue
        pr.ln_x(T["xnorm2"])
        pr.ffn()
        if lim == 2:
            kb.barrier(engs=("sync",))
            pr.store_x(tile)
            continue
        pr.ln_x(T["xnorm"])
        pr.in_proj("b")
        pr.dt_prep("b")
        pr.conv_phase("b")
        for ch in range(4):
            pr.ssd_chunk("b", ch)
        pr.norm_out("b")
        if lim == 3:
            kb.barrier(engs=("sync",))
            pr.store_x(tile)
            continue
        pr.ln_x(T["xnorm2"])
        pr.ffn()
        kb.barrier(engs=("sync",))
        pr.store_x(tile)
    pr.finish()
    kb.check_waits()
    PE_LABELS.append(list(kb.pe_labels))
    with nc.Block() as block:
        kb.finalize_into_block(block)
    return nc



def dump_pe_labels(upto="full"):
    import json
    nc, kb = None, None
    nc = build_program(upto)
    return PE_LABELS[-1]


PE_LABELS = []


def kernel(**inputs):
    upto = inputs.pop("_upto", "full")
    ncores = 8
    u = np.asarray(inputs["L0_attn_a_u"], dtype=np.float32)
    st = stage_weights(inputs)
    nc = build_program(upto)
    in_maps = []
    for c in range(ncores):
        m = dict(st)
        m["u"] = np.ascontiguousarray(u[c].T)
        in_maps.append(m)
    res = run_bass_kernel_spmd(nc, in_maps, core_ids=list(range(ncores)))
    outs = [np.ascontiguousarray(r["out"].T) for r in res.results]
    return np.stack(outs, axis=0)
